# version 03 (final)
# speedup vs baseline: 1.0495x; 1.0014x over previous
import numpy as np
import ml_dtypes
from contextlib import ExitStack
import concourse.bass as bass
import concourse.mybir as mybir
from concourse.bass_utils import run_bass_kernel_spmd

F32 = mybir.dt.float32
BF16 = mybir.dt.bfloat16
AF = mybir.ActivationFunctionType
ALU = mybir.AluOpType
AX = mybir.AxisListType

B, T, E = 4, 4096, 1024
NCORES = 8
LN_EPS = 1e-5
_bf16 = ml_dtypes.bfloat16


class Op:
    __slots__ = ("eng", "fn", "deps", "seq", "is_dma", "signaled", "cum")

    def __init__(self, eng, fn, deps, seq, is_dma):
        self.eng, self.fn, self.deps, self.seq, self.is_dma = eng, fn, deps, seq, is_dma
        self.signaled = False
        self.cum = None


class Sched:
    ENGINES = ("tensor", "vector", "scalar", "gpsimd", "sync")

    def __init__(self, nc):
        self.nc = nc
        self.ops = {e: [] for e in self.ENGINES}
        self.all = []

    def op(self, eng, fn, deps=()):
        o = Op(eng, fn, [d for d in deps if d is not None], len(self.all), False)
        self.ops[eng].append(o)
        self.all.append(o)
        return o

    def dma(self, eng, fn, deps=()):
        o = Op(eng, fn, [d for d in deps if d is not None], len(self.all), True)
        self.ops[eng].append(o)
        self.all.append(o)
        return o

    def emit(self, stack):
        nc = self.nc
        for o in self.all:
            if o.is_dma:
                o.signaled = True
            for d in o.deps:
                assert d.seq < o.seq, "dep must precede op"
                d.signaled = True
        terminals = []
        for e in self.ENGINES:
            if self.ops[e]:
                self.ops[e][-1].signaled = True
                terminals.append(self.ops[e][-1])
        csem, dsem = {}, {}
        for e in self.ENGINES:
            if any((not o.is_dma) and o.signaled for o in self.ops[e]):
                csem[e] = stack.enter_context(nc.semaphore(f"c_{e}"))
            if any(o.is_dma and o.signaled for o in self.ops[e]):
                dsem[e] = stack.enter_context(nc.semaphore(f"d_{e}"))
        for e in self.ENGINES:
            cc = cd = 0
            for o in self.ops[e]:
                if o.signaled:
                    if o.is_dma:
                        cd += 16
                    else:
                        cc += 1
                o.cum = cd if o.is_dma else cc
        sched = self
        import bisect
        dma_seqs = {e: [o.seq for o in self.ops[e] if o.is_dma] for e in self.ENGINES}
        dma_cums = {e: [o.cum for o in self.ops[e] if o.is_dma] for e in self.ENGINES}

        def dma_wait_val(eng_name, dep, waiter_seq):
            i = bisect.bisect_left(dma_seqs[eng_name], waiter_seq) - 1
            assert i >= 0
            return dma_cums[eng_name][i]

        def sem_of(d):
            return (dsem if d.is_dma else csem)[d.eng]

        settle_vals = {e: set() for e in self.ENGINES}
        for o in self.all:
            for d in o.deps:
                if d.is_dma:
                    settle_vals[d.eng].add(dma_wait_val(d.eng, d, o.seq))

        def make_body(e):
            def body(eng):
                waited = {}
                settled = set()
                prev_cum = 0
                for o in sched.ops[e]:
                    for d in o.deps:
                        sm = sem_of(d)
                        val = dma_wait_val(d.eng, d, o.seq) if d.is_dma else d.cum
                        if waited.get(id(sm), -1) < val:
                            eng.wait_ge(sm, val)
                            waited[id(sm)] = val
                    if o.is_dma:
                        if (prev_cum in settle_vals[e] and prev_cum not in settled
                                and prev_cum > 0):
                            eng.wait_ge(dsem[e], prev_cum)
                            settled.add(prev_cum)
                        prev_cum = o.cum
                    inst = o.fn(eng)
                    if o.signaled:
                        if inst is None:
                            raise RuntimeError(f"op #{o.seq} on {e} must return instruction")
                        inst.then_inc(sem_of(o), 16 if o.is_dma else 1)
                if e == "sync":
                    for t in terminals:
                        if t.eng == "sync" and not t.is_dma:
                            continue
                        eng.wait_ge(sem_of(t), t.cum)
                    for ee in sched.ENGINES:
                        dmas = [o for o in sched.ops[ee] if o.is_dma and o.signaled]
                        if dmas:
                            eng.wait_ge(dsem[ee], dmas[-1].cum)
            return body

        blk = stack.enter_context(nc.Block())
        if self.ops["tensor"]:
            blk.tensor(make_body("tensor"))
        if self.ops["vector"]:
            blk.vector(make_body("vector"))
        if self.ops["scalar"]:
            blk.scalar(make_body("scalar"))
        if self.ops["gpsimd"]:
            blk.gpsimd(make_body("gpsimd"))
        blk.sync(make_body("sync"))


class Buf:
    __slots__ = ("name", "w", "readers", "mm_cells")

    def __init__(self, name):
        self.name = name
        self.w = None
        self.readers = []
        self.mm_cells = {}


class Ctx:

    def __init__(self, name="l"):
        self.nc = bass.Bass("TRN2", target_bir_lowering=False)
        self.sch = Sched(self.nc)
        self.stack = ExitStack()
        self._bufs = {}

    def inp(self, name, shape, dtype=F32):
        return self.nc.dram_tensor(name, list(shape), dtype, kind="ExternalInput").ap()

    def outp(self, name, shape, dtype=F32):
        return self.nc.dram_tensor(name, list(shape), dtype, kind="ExternalOutput").ap()

    def dram(self, name, shape, dtype=F32):
        return self.nc.dram_tensor(name, list(shape), dtype, kind="Internal").ap()

    def sb(self, name, shape, dtype=F32):
        t = self.stack.enter_context(self.nc.sbuf_tensor(name, list(shape), dtype))
        self._bufs[id(t)] = Buf(name)
        return t

    def ps(self, name, shape, dtype=F32):
        t = self.stack.enter_context(self.nc.psum_tensor(name, list(shape), dtype))
        self._bufs[id(t)] = Buf(name)
        return t

    def buf(self, t):
        b = self._bufs.get(id(t))
        if b is None:
            b = Buf("anon")
            self._bufs[id(t)] = b
        return b

    def finish(self):
        self.sch.emit(self.stack)
        return self.nc

    def _emit(self, eng, fn, reads, writes, deps, is_dma):
        dd = list(deps)
        rbufs = [self.buf(t) for t in reads]
        wbufs = [self.buf(t) for t in writes]
        for b in rbufs:
            if b.mm_cells:
                for cell in b.mm_cells.values():
                    cell["stop"] = True
                b.mm_cells = {}
        for b in rbufs:
            if b.w is not None:
                dd.append(b.w)
        for b in wbufs:
            if b.w is not None:
                dd.append(b.w)
            dd.extend(b.readers)
        tok = (self.sch.dma if is_dma else self.sch.op)(eng, fn, dd)
        for b in wbufs:
            b.w = tok
            b.readers = []
        for b in rbufs:
            b.readers.append(tok)
        return tok

    def op(self, eng, fn, reads=(), writes=(), deps=()):
        return self._emit(eng, fn, reads, writes, deps, False)

    def dma(self, eng, fn, reads=(), writes=(), deps=()):
        return self._emit(eng, fn, reads, writes, deps, True)

    def ld(self, dst_ap, src_ap, dst_t, eng="sync", deps=()):
        return self.dma(eng, lambda e: e.dma_start(out=dst_ap, in_=src_ap),
                        writes=[dst_t], deps=deps)

    def sbcp(self, dst_ap, src_ap, dst_t, src_t, eng="sync", deps=()):
        return self.dma(eng, lambda e: e.dma_start(out=dst_ap, in_=src_ap),
                        reads=[src_t], writes=[dst_t], deps=deps)

    def st(self, dst_ap, src_ap, src_t, eng="sync", deps=()):
        return self.dma(eng, lambda e: e.dma_start(out=dst_ap, in_=src_ap),
                        reads=[src_t], deps=deps)

    def mm(self, out_ap, lhsT_ap, rhs_ap, out_t, in_ts, start, stop, deps=()):
        assert lhsT_ap.base_partition() == rhs_ap.base_partition(), (
            f"mm base mismatch {lhsT_ap.name}@{lhsT_ap.base_partition()} vs "
            f"{rhs_ap.name}@{rhs_ap.base_partition()}")
        cell = {"stop": stop}
        tok = self.op("tensor",
                      lambda e: e.matmul(out_ap, lhsT_ap, rhs_ap, start=start,
                                         stop=cell["stop"]),
                      reads=list(in_ts), writes=[out_t], deps=deps)
        import concourse.mybir as _mb
        _reg = (int(out_ap.offset) * _mb.dt.size(out_ap.dtype)) // 2048
        self.buf(out_t).mm_cells[(out_ap.base_partition(), _reg)] = cell
        return tok


def make_consts():
    c = {}
    c["iden"] = np.eye(128, dtype=np.float32).astype(_bf16)
    c["idenf"] = np.eye(128, dtype=np.float32)
    c["onesr"] = np.ones((1, 512), dtype=np.float32).astype(_bf16)
    c["onesc"] = np.ones((128, 1), dtype=np.float32).astype(_bf16)
    jl = np.arange(128)[:, None]
    ii = np.arange(128)[None, :]
    for off in (-128, 0, 128):
        c[f"mstep_{off}"] = ((off + jl - ii) >= 4).astype(np.float32).astype(_bf16)
        for d in range(-3, 4):
            m = ((off + jl - ii) == d)
            if m.any():
                c[f"mz_{off}_{d}"] = m.astype(np.float32).astype(_bf16)
    return c


CONSTS = make_consts()
A0_ZM = [(off, d) for off in (-128, 0, 128) for d in range(-3, 4)
         if f"mz_{off}_{d}" in CONSTS]


class G:
    pass


def emit_consts(ctx, names=("iden", "idenf", "onesr", "onesc")):
    g = G()
    for n in names:
        arr = CONSTS[n]
        dt = BF16 if arr.dtype == _bf16 else F32
        ap = ctx.inp("c_" + n, arr.shape, dt)
        t = ctx.sb("g_" + n, arr.shape, dt)
        ctx.ld(t[:], ap, t)
        setattr(g, n, t)
    g.epsc = ctx.sb("g_epsc", [128, 1], F32)
    g.zc = ctx.sb("g_zc", [128, 1], F32)
    ctx.op("vector", lambda e: e.memset(g.epsc[:], LN_EPS), writes=[g.epsc])
    ctx.op("vector", lambda e: e.memset(g.zc[:], 0.0), writes=[g.zc])
    g.feed = {("c_" + n): CONSTS[n] for n in names}
    return g


def emit_ln(ctx, g, xln, ntok, pool):
    nts = ntok // 128
    assert 2 * nts <= 128 and ntok % 128 == 0
    pgram = [pool["pa"], pool["pb"]]
    pstat = pool["pc"]
    ptb = pool["pbf"]
    statS = pool.get("statS") or ctx.sb("ln_statS", [128, 2 * nts], F32)
    dtile = pool.get("dtile") or ctx.sb("ln_dtile", [128, 128], F32)
    P = pool.get("P") or ctx.sb("ln_P", [128, 2 * nts], BF16)
    Ts = pool.get("Ts") or ctx.sb("ln_Ts", [2 * nts, 128], BF16)
    t1 = pool.get("t1") or ctx.sb("ln_t1", [128, nts], F32)
    t2 = pool.get("t2") or ctx.sb("ln_t2", [128, nts], F32)
    rows = pool["rows"]
    bps = [pool["pd"], pool["pe"]]

    for ts in range(nts):
        pg = pgram[ts % 2]
        for cc in range(8):
            lhs = xln[cc][:, ts * 128:(ts + 1) * 128]
            ctx.mm(pg[:, 0:128], lhs, lhs, pg, [xln[cc]], start=(cc == 0), stop=(cc == 7))
            ctx.mm(pstat[:, ts:ts + 1], lhs, g.onesc[:], pstat, [xln[cc], g.onesc],
                   start=(ts == 0 and cc == 0), stop=(ts == nts - 1 and cc == 7))
        ctx.op("vector", lambda e, pg=pg: e.tensor_tensor(
            out=dtile[:], in0=pg[:, 0:128], in1=g.idenf[:], op=ALU.mult),
            reads=[pg, g.idenf], writes=[dtile])
        ctx.op("vector", lambda e, ts=ts: e.reduce_sum(
            out=statS[:, nts + ts:nts + ts + 1], in_=dtile[:], axis=AX.X),
            reads=[dtile], writes=[statS])
    ctx.op("vector", lambda e: e.tensor_copy(out=statS[:, 0:nts], in_=pstat[:, 0:nts]),
           reads=[pstat], writes=[statS])
    ctx.op("vector", lambda e: e.tensor_scalar_mul(out=t1[:], in0=statS[:, 0:nts],
                                                   scalar1=1.0 / 1024.0),
           reads=[statS], writes=[t1])
    ctx.op("vector", lambda e: e.tensor_tensor(out=t2[:], in0=t1[:], in1=t1[:], op=ALU.mult),
           reads=[t1], writes=[t2])
    ctx.op("vector", lambda e: e.scalar_tensor_tensor(
        out=t2[:], in0=statS[:, nts:2 * nts], scalar=1.0 / 1024.0, in1=t2[:],
        op0=ALU.mult, op1=ALU.subtract), reads=[statS], writes=[t2])
    ctx.op("scalar", lambda e: e.activation(out=t2[:], in_=t2[:], func=AF.Ln,
                                            bias=g.epsc[:, 0:1], scale=1.0),
           reads=[g.epsc], writes=[t2])
    ctx.op("scalar", lambda e: e.activation(out=t2[:], in_=t2[:], func=AF.Exp,
                                            bias=g.zc[:, 0:1], scale=-0.5),
           reads=[g.zc], writes=[t2])
    ctx.op("vector", lambda e: e.tensor_tensor(out=P[:, 0:nts], in0=t1[:], in1=t2[:],
                                               op=ALU.mult), reads=[t1, t2], writes=[P])
    ctx.op("vector", lambda e: e.tensor_copy(out=P[:, nts:2 * nts], in_=t2[:]),
           reads=[t2], writes=[P])
    ctx.op("tensor", lambda e: e.transpose(ptb[0:2 * nts, 0:128], P[:], g.iden[:]),
           reads=[P, g.iden], writes=[ptb])
    ctx.op("vector", lambda e: e.tensor_copy(out=Ts[:], in_=ptb[0:2 * nts, 0:128]),
           reads=[ptb], writes=[Ts])
    ctx.sbcp(rows[0:1, :].rearrange("o (a q) -> o a q", a=nts), Ts[0:nts, :], rows, Ts)
    ctx.sbcp(rows[32:33, :].rearrange("o (a q) -> o a q", a=nts), Ts[nts:2 * nts, :], rows, Ts)
    pdm, pem = bps
    for i0 in range(0, ntok, 512):
        w_ = min(512, ntok - i0)
        sl = slice(i0, i0 + w_)
        ctx.mm(pdm[:, 0:w_], pool["onesP"][32:33, 0:128], rows[32:33, sl], pdm,
               [pool["onesP"], rows], start=True, stop=True)
        ctx.mm(pem[:, 0:w_], g.onesr[0:1, 0:128], rows[0:1, sl], pem, [g.onesr, rows],
               start=True, stop=True)
        for cc in range(8):
            ctx.op("vector", lambda e, cc=cc, sl=sl, w_=w_: e.tensor_tensor(
                out=xln[cc][:, sl], in0=xln[cc][:, sl], in1=pdm[:, 0:w_], op=ALU.mult),
                reads=[xln[cc], pdm], writes=[xln[cc]])
            ctx.op("vector", lambda e, cc=cc, sl=sl, w_=w_: e.tensor_tensor(
                out=xln[cc][:, sl], in0=xln[cc][:, sl], in1=pem[:, 0:w_], op=ALU.subtract),
                reads=[xln[cc], pem], writes=[xln[cc]])
    aff = pool.get("affine")
    if aff is not None:
        gcols, bcols = aff
        for cc in range(8):
            ctx.op("vector", lambda e, cc=cc: e.tensor_scalar(
                out=xln[cc][:], in0=xln[cc][:], scalar1=gcols[:, cc:cc + 1],
                scalar2=bcols[:, cc:cc + 1], op0=ALU.mult, op1=ALU.add),
                reads=[xln[cc], gcols, bcols], writes=[xln[cc]])
    return t1, t2


A0_SW = 392
A0_ROWSTRIDE = 32 * A0_SW
A0_SCR_N = 7 * 262144


def build_attn_a0(nheads=8, nq=16, relpos=True):
    from concourse.ap import AP
    ctx = Ctx("a0")
    nc = ctx.nc
    xT = ctx.inp("xT", (E, T))
    wq = ctx.inp("wq", (E, 512))
    wk = ctx.inp("wk", (E, 512))
    wv = ctx.inp("wv", (E, 512))
    bq = ctx.inp("bq", (1, 512))
    bk = ctx.inp("bk", (1, 512))
    bv = ctx.inp("bv", (1, 512))
    woR = ctx.inp("woR", (512, E))
    boR = ctx.inp("boR", (1, E))
    if relpos:
        tkT = ctx.inp("tkT", (64, 9))
        tv0 = ctx.inp("tv0", (1, 64))
        tvsh = ctx.inp("tvsh", (8, 64))
    part = ctx.outp("part", (E, T))
    scr = ctx.dram("scr", (A0_SCR_N,), BF16) if relpos else None
    scrbuf = object()

    g = emit_consts(ctx)
    mstep = {}
    mz = {}
    for off in ((-128, 0, 128) if relpos else ()):
        ap = ctx.inp(f"c_mstep_{off}", (128, 128), BF16)
        t = ctx.sb(f"g_mstep_{off}", (128, 128), BF16)
        ctx.ld(t[:], ap, t)
        mstep[off] = t
        g.feed[f"c_mstep_{off}"] = CONSTS[f"mstep_{off}"]
    for (off, d) in (A0_ZM if relpos else ()):
        ap = ctx.inp(f"c_mz_{off}_{d}", (128, 128), BF16)
        t = ctx.sb(f"g_mz_{off}_{d}", (128, 128), BF16)
        ctx.ld(t[:], ap, t)
        mz[(off, d)] = t
        g.feed[f"c_mz_{off}_{d}"] = CONSTS[f"mz_{off}_{d}"]

    pa = ctx.ps("pa", (128, 512))
    pb = ctx.ps("pb", (128, 512))
    pc = ctx.ps("pc", (128, 512))
    pd = ctx.ps("pd", (128, 512))
    pe = ctx.ps("pe", (128, 512))
    p2 = ctx.ps("p2", (128, 1024))
    pbf = ctx.ps("pbf", (128, 256), BF16)

    xln = [ctx.sb(f"xln{c}", (128, T), BF16) for c in range(8)]
    YT = [ctx.sb(f"YT{i}", (128, T), BF16) for i in range(4)]
    QT = ctx.sb("QT", (64, T), BF16)
    KT = ctx.sb("KT", (64, T), BF16)
    rows5 = ctx.sb("rows5", (65, T), BF16)
    Vh = ctx.sb("Vh", (128, 32 * 65), BF16)
    cTb = ctx.sb("cTb", (8, T), BF16)
    qc = ctx.sb("qc", (128, 256), BF16)
    Bb = [ctx.sb(f"Bb{i}", (128, 768), BF16) for i in range(2)]
    Eb = [ctx.sb(f"Eb{i}", (128, 1024), BF16) for i in range(2)]
    em = ctx.sb("em", (128, 128), BF16)
    ez = ctx.sb("ez", (128, 128), BF16)
    afin = ctx.sb("afin", (65, 512))
    ast_s = ctx.sb("ast_s", (65, 512))
    zsb = ctx.sb("zsb", (33, 896))
    zkm = ctx.sb("zkm", (7, 256), BF16)
    rl_f = ctx.sb("rl_f", (65, 256))
    rl_b = ctx.sb("rl_b", (65, 256), BF16)
    lrow_b = ctx.sb("lrow_b", (65, 256), BF16)
    rrow_f = ctx.sb("rrow_f", (65, 256))
    rrow_b = ctx.sb("rrow_b", (65, 256), BF16)
    ytmp = ctx.sb("ytmp", (64, 256))
    wqt = ctx.sb("wqt", (128, 512), BF16)
    wkt = ctx.sb("wkt", (128, 512), BF16)
    wvt = ctx.sb("wvt", (128, 512), BF16)
    bqb = ctx.sb("bqb", (1, 512), BF16)
    bkb = ctx.sb("bkb", (1, 512), BF16)
    bvb = ctx.sb("bvb", (1, 512), BF16)
    tkb = ctx.sb("tkb", (64, 9), BF16)
    tkf = ctx.sb("tkf", (64, 9))
    tkd = ctx.sb("tkd", (64, 8), BF16)
    tv0b = ctx.sb("tv0b", (65, 64), BF16)
    tvshb = ctx.sb("tvshb", (8, 64), BF16)
    tvdb = ctx.sb("tvdb", (8, 64), BF16)
    tv8m0 = ctx.sb("tv8m0", (65, 64), BF16)
    wot = [ctx.sb(f"wot{i}", (128, 1024), BF16) for i in range(4)]
    bob = ctx.sb("bob", (1, 1024), BF16)
    obuf = [ctx.sb(f"obuf{i}", (128, 512)) for i in range(2)]

    ctx.ld(bqb[:], bq, bqb, eng="gpsimd")
    ctx.ld(bkb[:], bk, bkb, eng="gpsimd")
    ctx.ld(bvb[:], bv, bvb, eng="gpsimd")
    if relpos:
        ctx.ld(tkb[:], tkT, tkb, eng="gpsimd")
        ctx.ld(tkf[:], tkT, tkf, eng="sync")
        ctx.ld(tv0b[64:65, :], tv0, tv0b, eng="gpsimd")
        ctx.ld(tv0b[0:1, :], tv0, tv0b, eng="gpsimd")
        ctx.ld(tvshb[:], tvsh, tvshb, eng="gpsimd")
    ctx.ld(bob[:], boR, bob, eng="gpsimd")
    for cc in range(4):
        ctx.ld(wot[cc][:], woR.rearrange("(c p) f -> c p f", p=128)[cc],
               wot[cc], eng="gpsimd")
    if relpos:
        ctx.op("vector", lambda e: e.tensor_scalar(
            out=tkd[:], in0=tkb[:, 1:9], scalar1=tkf[:, 0:1], scalar2=None,
            op0=ALU.subtract), reads=[tkb, tkf], writes=[tkd])
        ctx.mm(pe[0:8, 0:64], g.onesr[0:1, 0:8], tv0b[0:1, :], pe, [g.onesr, tv0b],
               start=True, stop=True)
        ctx.op("vector", lambda e: e.tensor_tensor(
            out=tvdb[:], in0=tvshb[:], in1=pe[0:8, 0:64], op=ALU.subtract),
            reads=[tvshb, pe], writes=[tvdb])
        ctx.sbcp(tv8m0[64:65, :], tvdb[7:8, :], tv8m0, tvdb)
        zt = xln[0]
        ctx.op("vector", lambda e: e.memset(zt[:, 0:2048], 0.0), writes=[zt])
        for k in range(7):
            ctx.dma("sync", lambda e, k=k: e.dma_start(
                out=scr[k * 262144:(k + 1) * 262144].rearrange("(p q) -> p q", q=2048),
                in_=zt[:, 0:2048]), reads=[zt], writes=[scrbuf])
    ctx.op("vector", lambda e: e.memset(Vh[:, 64:32 * 65:65], 1.0), writes=[Vh])
    onesP = ctx.sb("onesP", (65, 512), BF16)
    ctx.op("vector", lambda e: e.memset(onesP[:], 1.0), writes=[onesP])
    if nheads < 8:
        for yt in YT:
            ctx.op("vector", lambda e, yt=yt: e.memset(yt[:], 0.0), writes=[yt])

    for cc in range(8):
        ctx.ld(xln[cc][:], xT[cc * 128:(cc + 1) * 128, :], xln[cc], eng="gpsimd")
    emit_ln(ctx, g, xln, T,
            dict(pa=pa, pb=pb, pc=pc, pd=pd, pe=pe, p2=p2, pbf=pbf, rows=rows5,
                 onesP=onesP))

    started = {}

    def accmm(out_ap, lhsT, rhs, in_ts):
        bank = out_ap.tensor
        import concourse.mybir as _mb
        reg = (int(out_ap.offset) * _mb.dt.size(out_ap.dtype)) // 2048
        key = (id(bank), out_ap.base_partition(), reg)
        st = not started.get(key, False)
        started[key] = True
        return ctx.mm(out_ap, lhsT, rhs, bank, in_ts, start=st, stop=False)

    def reset(bank):
        for k in [k for k in started if k[0] == id(bank)]:
            started[k] = False

    for h in range(nheads):
        hs = slice(64 * h, 64 * h + 64)
        ctx.ld(wqt[:], wq.rearrange("(c p) f -> p c f", p=128)[:, :, hs], wqt, eng="gpsimd")
        ctx.ld(wkt[:], wk.rearrange("(c p) f -> p c f", p=128)[:, :, hs], wkt, eng="gpsimd")
        ctx.ld(wvt[:], wv.rearrange("(c p) f -> p c f", p=128)[:, :, hs], wvt, eng="gpsimd")
        for (wt, bb, r0q) in ((wqt, bqb, 0), (wkt, bkb, 64)):
            for nt in range(8):
                tsl = slice(nt * 512, nt * 512 + 512)
                reset(pe)
                for cc in range(8):
                    accmm(pe[0:64, :], wt[:, cc * 64:(cc + 1) * 64], xln[cc][:, tsl],
                          [wt, xln[cc]])
                accmm(pe[0:64, :], bb[0:1, hs], g.onesr[0:1, 0:512], [bb, g.onesr])
                eng = "vector" if nt % 2 == 0 else "scalar"
                dst = QT if r0q == 0 else KT
                if eng == "vector":
                    ctx.op(eng, lambda e, dst=dst, tsl=tsl: e.tensor_copy(
                        out=dst[:, tsl], in_=pe[0:64, :]), reads=[pe], writes=[dst])
                else:
                    ctx.op(eng, lambda e, dst=dst, tsl=tsl: e.copy(
                        out=dst[:, tsl], in_=pe[0:64, :]), reads=[pe], writes=[dst])
        for tc in range(32):
            tsl = slice(tc * 128, tc * 128 + 128)
            reset(pd)
            for cc in range(8):
                accmm(pd[:, 0:64], xln[cc][:, tsl], wvt[:, cc * 64:(cc + 1) * 64],
                      [xln[cc], wvt])
            accmm(pd[:, 0:64], g.onesr[0:1, 0:128], bvb[0:1, hs], [g.onesr, bvb])
            ctx.op("vector", lambda e, tc=tc: e.tensor_copy(
                out=Vh[:, tc * 65:tc * 65 + 64], in_=pd[:, 0:64]),
                reads=[pd], writes=[Vh])
        for nt in (range(8) if relpos else ()):
            tsl = slice(nt * 512, nt * 512 + 512)
            reset(pc)
            accmm(pc[0:8, :], tkd[:], QT[:, tsl], [tkd, QT])
            accmm(pc[32:33, :], tkb[:, 0:1], QT[:, tsl], [tkb, QT])
            accmm(pc[64:65, :], tkb[:, 8:9], QT[:, tsl], [tkb, QT])
            ctx.op("scalar", lambda e, tsl=tsl: e.copy(
                out=cTb[:, tsl], in_=pc[0:8, :]), reads=[pc], writes=[cTb])
            ctx.op("vector", lambda e, tsl=tsl: e.tensor_copy(
                out=rows5[32:33, tsl], in_=pc[32:33, :]), reads=[pc], writes=[rows5])
            ctx.op("vector", lambda e, tsl=tsl: e.tensor_copy(
                out=rows5[64:65, tsl], in_=pc[64:65, :]), reads=[pc], writes=[rows5])
        if relpos:
            ctx.sbcp(rows5[0:1, :], cTb[7:8, :], rows5, cTb)
            ctx.op("scalar", lambda e: e.activation(
                out=rows5[0:1, :], in_=rows5[0:1, :], func=AF.Exp, bias=g.zc[0:1, 0:1],
                scale=0.125), reads=[g.zc], writes=[rows5])
            ctx.op("vector", lambda e: e.tensor_scalar_add(
                out=rows5[0:1, :], in0=rows5[0:1, :], scalar1=-1.0),
                reads=[rows5], writes=[rows5])
        for stq in (range(4) if relpos else ()):
            for j in range(8):
                st_i = stq * 8 + j
                ctx.op("tensor", lambda e, st_i=st_i: e.transpose(
                    pbf[0:128, 0:8], cTb[0:8, st_i * 128:(st_i + 1) * 128], g.iden[0:8, 0:8]),
                    reads=[cTb, g.iden], writes=[pbf])
                ctx.op("vector", lambda e, st_i=st_i: e.tensor_copy(
                    out=qc[:, (st_i % 8) * 8 + 64 * 0:(st_i % 8) * 8 + 8] if False
                    else qc[:, (st_i % 8) * 8:(st_i % 8) * 8 + 8], in_=pbf[0:128, 0:8]),
                    reads=[pbf], writes=[qc])
            st0 = stq * 8
            out_ap = AP(scr.tensor, 125 + st0 * A0_SW,
                        [[A0_ROWSTRIDE + 1, 128], [A0_SW, 8], [1, 7]])
            in_ap = qc[:, 0:64].rearrange("p (s r) -> p s r", s=8)[:, :, 0:7]
            ctx.dma("gpsimd", lambda e, o=out_ap, i=in_ap: e.dma_start(out=o, in_=i),
                    reads=[qc], writes=[scrbuf])
        for q in range(nq):
            for s in ((0, 1) if relpos else ()):
                st_i = 2 * q + s
                in_ap = AP(scr.tensor, st_i * A0_SW, [[A0_ROWSTRIDE, 128], [1, 384]])
                ctx.dma("sync", lambda e, o=Bb[q % 2][:, s * 384:s * 384 + 384],
                        i=in_ap: e.dma_start(out=o, in_=i),
                        reads=[scrbuf], writes=[Bb[q % 2]])
            reset(pa)
            reset(pb)
            reset(pc)
            reset(pd)
            zped = False
            for grp in range(8):
                reset(p2)
                Ecur = Eb[grp % 2]
                chunks = [grp * 4 + i for i in range(4)]
                for ci, c in enumerate(chunks):
                    csl = slice(ci * 256, ci * 256 + 256)
                    accmm(p2[:, csl], KT[:, c * 128:(c + 1) * 128],
                          QT[:, q * 256:q * 256 + 256], [KT, QT])
                    cls = [None, None]
                    for s in (0, 1):
                        delta = c - (2 * q + s)
                        cls[s] = ("L" if delta <= -2 else
                                  ("R" if delta >= 2 else "G"))
                    if not relpos:
                        continue
                    if cls[0] == cls[1] or (cls[0] != "R" and cls[1] != "R"):
                        r0b = 64 if cls[0] == "R" else 32
                        accmm(p2[:, csl], onesP[r0b:r0b + 1, 0:128],
                              rows5[r0b:r0b + 1, q * 256:q * 256 + 256], [onesP, rows5])
                    else:
                        for s in (0, 1):
                            r0b = 64 if cls[s] == "R" else 32
                            qq = slice(q * 256 + s * 128, q * 256 + s * 128 + 128)
                            accmm(p2[:, ci * 256 + s * 128:ci * 256 + s * 128 + 128],
                                  onesP[r0b:r0b + 1, 0:128], rows5[r0b:r0b + 1, qq], [onesP, rows5])
                    for s in (0, 1):
                        if cls[s] == "G":
                            delta = c - (2 * q + s)
                            bsl = slice(s * 384 + (delta + 1) * 128,
                                        s * 384 + (delta + 1) * 128 + 128)
                            accmm(p2[:, ci * 256 + s * 128:ci * 256 + s * 128 + 128],
                                  Bb[q % 2][:, bsl], g.iden[:], [Bb[q % 2], g.iden])
                ctx.op("scalar", lambda e, Ecur=Ecur: e.activation(
                    out=Ecur[:], in_=p2[:], func=AF.Exp, bias=g.zc[:, 0:1], scale=0.125),
                    reads=[p2, g.zc], writes=[Ecur])
                for ci, c in enumerate(chunks):
                    vsl = Vh[:, c * 65:c * 65 + 65]
                    cls = []
                    for s in (0, 1):
                        delta = c - (2 * q + s)
                        if relpos:
                            cls.append("L" if delta <= -2 else ("R" if delta >= 2 else "G"))
                        else:
                            cls.append("L")
                    if cls[0] == cls[1]:
                        base_col = 256 if cls[0] == "R" else 0
                        esl = Ecur[:, ci * 256:ci * 256 + 256]
                        accmm(pa[0:65, base_col:base_col + 256], vsl, esl,
                              [Vh, Ecur])
                    else:
                        for s in (0, 1):
                            base_col = 256 if cls[s] == "R" else 0
                            esl = Ecur[:, ci * 256 + s * 128:ci * 256 + s * 128 + 128]
                            accmm(pa[0:65, base_col + s * 128:base_col + s * 128 + 128],
                                  vsl, esl, [Vh, Ecur])
                    for s in (0, 1):
                        if cls[s] != "G":
                            continue
                        delta = c - (2 * q + s)
                        off = delta * 128
                        esl = Ecur[:, ci * 256 + s * 128:ci * 256 + s * 128 + 128]
                        ctx.op("vector", lambda e, esl=esl, off=off: e.tensor_tensor(
                            out=em[:], in0=esl, in1=mstep[off][:], op=ALU.mult),
                            reads=[Ecur, mstep[off]], writes=[em])
                        accmm(pb[0:65, s * 128:s * 128 + 128], vsl, em[:], [Vh, em])
                        for d in range(-3, 4):
                            if (off, d) not in mz:
                                continue
                            dd = d + 3
                            ctx.op("vector", lambda e, esl=esl, off=off, d=d:
                                   e.tensor_tensor(out=ez[:], in0=esl,
                                                   in1=mz[(off, d)][:], op=ALU.mult),
                                   reads=[Ecur, mz[(off, d)]], writes=[ez])
                            bank = pc if dd < 4 else pd
                            col = (dd % 4) * 128
                            accmm(bank[32 * s:32 * s + 1, col:col + 128],
                                  g.onesc[:], ez[:], [g.onesc, ez])
            qsl = slice(q * 256, q * 256 + 256)
            if not relpos:
                ctx.op("vector", lambda e: e.tensor_copy(
                    out=afin[:, 0:256], in_=pa[0:65, 0:256]), reads=[pa], writes=[afin])
                ctx.op("vector", lambda e: e.reciprocal(
                    out=rl_f[64:65, :], in_=afin[64:65, 0:256]), reads=[afin], writes=[rl_f])
                ctx.op("vector", lambda e: e.tensor_copy(
                    out=rl_b[64:65, :], in_=rl_f[64:65, :]), reads=[rl_f], writes=[rl_b])
                reset(pe)
                accmm(pe[0:64, 256:512], onesP[64:65, 0:64], rl_b[64:65, :], [onesP, rl_b])
                yt = YT[h // 2]
                r0 = 64 * (h % 2)
                ctx.op("vector", lambda e, yt=yt, r0=r0, qsl=qsl: e.tensor_tensor(
                    out=yt[r0:r0 + 64, qsl], in0=afin[0:64, 0:256], in1=pe[0:64, 256:512],
                    op=ALU.mult), reads=[afin, pe], writes=[yt])
                continue
            ctx.op("vector", lambda e: e.tensor_copy(out=afin[:, :], in_=pa[0:65, :]),
                   reads=[pa], writes=[afin])
            ctx.op("vector", lambda e: e.tensor_copy(out=ast_s[:, 0:256], in_=pb[0:65, 0:256]),
                   reads=[pb], writes=[ast_s])
            for r0z in (0, 32):
                ctx.op("vector", lambda e, r0z=r0z: e.tensor_copy(
                    out=zsb[r0z:r0z + 1, 0:512], in_=pc[r0z:r0z + 1, :]),
                    reads=[pc], writes=[zsb])
                ctx.op("vector", lambda e, r0z=r0z: e.tensor_copy(
                    out=zsb[r0z:r0z + 1, 512:896], in_=pd[r0z:r0z + 1, 0:384]),
                    reads=[pd], writes=[zsb])
            reset(pe)
            accmm(pe[0:65, 0:256], g.onesr[0:1, 0:65], rows5[0:1, qsl], [g.onesr, rows5])
            ctx.op("vector", lambda e: e.tensor_tensor(
                out=ast_s[:, 256:512], in0=ast_s[:, 0:256], in1=pe[0:65, 0:256],
                op=ALU.mult), reads=[ast_s, pe], writes=[ast_s])
            ctx.op("vector", lambda e: e.tensor_tensor(
                out=rrow_f[64:65, :], in0=afin[64:65, 256:512], in1=ast_s[64:65, 0:256],
                op=ALU.add), reads=[afin, ast_s], writes=[rrow_f])
            ctx.op("vector", lambda e: e.tensor_tensor(
                out=rrow_f[64:65, :], in0=rrow_f[64:65, :], in1=ast_s[64:65, 256:512],
                op=ALU.add), reads=[rrow_f, ast_s], writes=[rrow_f])
            ctx.op("vector", lambda e: e.tensor_copy(out=rrow_b[64:65, :], in_=rrow_f[64:65, :]),
                   reads=[rrow_f], writes=[rrow_b])
            ctx.op("vector", lambda e: e.tensor_tensor(
                out=afin[:, 0:256], in0=afin[:, 0:256], in1=afin[:, 256:512],
                op=ALU.add), reads=[afin], writes=[afin])
            ctx.op("vector", lambda e: e.tensor_tensor(
                out=afin[:, 0:256], in0=afin[:, 0:256], in1=ast_s[:, 256:512],
                op=ALU.add), reads=[afin, ast_s], writes=[afin])
            ctx.op("vector", lambda e: e.tensor_copy(out=lrow_b[64:65, :], in_=afin[64:65, 0:256]),
                   reads=[afin], writes=[lrow_b])
            ctx.sbcp(zkm[0:7, 0:128],
                     zsb[0:1, :].rearrange("o (a q) -> o a q", a=7),
                     zkm, zsb, eng="gpsimd")
            ctx.sbcp(zkm[0:7, 128:256],
                     zsb[32:33, :].rearrange("o (a q) -> o a q", a=7),
                     zkm, zsb, eng="gpsimd")
            ctx.mm(pb[0:64, 256:512], tv0b[64:65, :], lrow_b[64:65, :], pb,
                   [tv0b, lrow_b], start=True, stop=False)
            ctx.mm(pb[0:64, 256:512], tvdb[0:7, :], zkm[:, :], pb,
                   [tvdb, zkm], start=False, stop=False)
            ctx.mm(pb[0:64, 256:512], tv8m0[64:65, :], rrow_b[64:65, :], pb,
                   [tv8m0, rrow_b], start=False, stop=True)
            reset(pb)
            ctx.op("vector", lambda e: e.reciprocal(out=rl_f[64:65, :], in_=afin[64:65, 0:256]),
                   reads=[afin], writes=[rl_f])
            ctx.op("vector", lambda e: e.tensor_copy(out=rl_b[64:65, :], in_=rl_f[64:65, :]),
                   reads=[rl_f], writes=[rl_b])
            reset(pe)
            accmm(pe[0:64, 256:512], onesP[64:65, 0:64], rl_b[64:65, :], [onesP, rl_b])
            ctx.op("vector", lambda e: e.tensor_tensor(
                out=ytmp[:], in0=afin[0:64, 0:256], in1=pb[0:64, 256:512],
                op=ALU.add), reads=[afin, pb], writes=[ytmp])
            yt = YT[h // 2]
            r0 = 64 * (h % 2)
            ctx.op("vector", lambda e, yt=yt, r0=r0, qsl=qsl: e.tensor_tensor(
                out=yt[r0:r0 + 64, qsl], in0=ytmp[:], in1=pe[0:64, 256:512],
                op=ALU.mult), reads=[ytmp, pe], writes=[yt])
    for fc in range(8):
        fsl = slice(fc * 128, fc * 128 + 128)
        for nt in range(8):
            tsl = slice(nt * 512, nt * 512 + 512)
            bank = pa if nt % 2 == 0 else pb
            reset(bank)
            for cc in range(4):
                accmm(bank[:, :], wot[cc][:, fsl], YT[cc][:, tsl], [wot[cc], YT[cc]])
            accmm(bank[:, :], bob[0:1, fsl], g.onesr[0:1, 0:512], [bob, g.onesr])
            ob = obuf[nt % 2]
            ctx.op("vector", lambda e, bank=bank, ob=ob: e.tensor_copy(
                out=ob[:], in_=bank[:]), reads=[bank], writes=[ob])
            ctx.st(part[fsl, tsl], ob[:], ob, eng="sync")
    return ctx, g


def _np_bf16(a):
    return np.ascontiguousarray(a).astype(_bf16)


_A0_CACHE = {}


def run_attn_a0(x, Wq, bq, Wk, bk, Wv, bv, Wo, bo, table_k, table_v, nheads=8):
    if "a0" not in _A0_CACHE:
        ctx, g = build_attn_a0(nheads)
        nc = ctx.finish()
        _A0_CACHE["a0"] = (nc, g)
    nc, g = _A0_CACHE["a0"]
    zeros_b = np.zeros((1, E), np.float32)
    in_maps = []
    for c in range(NCORES):
        b, hh = c // 2, c % 2
        m = dict(g.feed)
        m["xT"] = x[b] if isinstance(x, list) else np.ascontiguousarray(x[b].T)
        m["wq"] = np.ascontiguousarray(Wq[:, 512 * hh:512 * hh + 512])
        m["wk"] = np.ascontiguousarray(Wk[:, 512 * hh:512 * hh + 512])
        m["wv"] = np.ascontiguousarray(Wv[:, 512 * hh:512 * hh + 512])
        m["bq"] = np.ascontiguousarray(bq.reshape(1, E)[:, 512 * hh:512 * hh + 512])
        m["bk"] = np.ascontiguousarray(bk.reshape(1, E)[:, 512 * hh:512 * hh + 512])
        m["bv"] = np.ascontiguousarray(bv.reshape(1, E)[:, 512 * hh:512 * hh + 512])
        m["woR"] = np.ascontiguousarray(Wo[512 * hh:512 * hh + 512, :])
        m["boR"] = bo.reshape(1, E) if hh == 0 else zeros_b
        m["tkT"] = np.ascontiguousarray(table_k.T)
        m["tv0"] = np.ascontiguousarray(table_v[0:1, :])
        m["tvsh"] = np.ascontiguousarray(table_v[1:9, :])
        in_maps.append(m)
    res = run_bass_kernel_spmd(nc, in_maps, core_ids=list(range(NCORES)))
    return [np.asarray(r["part"], np.float32) for r in res.results]


def run_attn_b0(x, Wq, bq, Wk, bk, Wv, bv, Wo, bo):
    if "b0" not in _A0_CACHE:
        ctx, g = build_attn_a0(relpos=False)
        nc = ctx.finish()
        _A0_CACHE["b0"] = (nc, g)
    nc, g = _A0_CACHE["b0"]
    zeros_b = np.zeros((1, E), np.float32)
    in_maps = []
    for c in range(NCORES):
        b, hh = c // 2, c % 2
        m = dict(g.feed)
        m["xT"] = x[b] if isinstance(x, list) else np.ascontiguousarray(x[b].T)
        m["wq"] = np.ascontiguousarray(Wq[:, 512 * hh:512 * hh + 512])
        m["wk"] = np.ascontiguousarray(Wk[:, 512 * hh:512 * hh + 512])
        m["wv"] = np.ascontiguousarray(Wv[:, 512 * hh:512 * hh + 512])
        m["bq"] = np.ascontiguousarray(bq.reshape(1, E)[:, 512 * hh:512 * hh + 512])
        m["bk"] = np.ascontiguousarray(bk.reshape(1, E)[:, 512 * hh:512 * hh + 512])
        m["bv"] = np.ascontiguousarray(bv.reshape(1, E)[:, 512 * hh:512 * hh + 512])
        m["woR"] = np.ascontiguousarray(Wo[512 * hh:512 * hh + 512, :])
        m["boR"] = bo.reshape(1, E) if hh == 0 else zeros_b
        in_maps.append(m)
    res = run_bass_kernel_spmd(nc, in_maps, core_ids=list(range(NCORES)))
    return [np.asarray(r["part"], np.float32) for r in res.results]


def rope_tables():
    half = 64
    inv = 1.0 / 10000.0 ** (np.arange(half, dtype=np.float64) / half)
    ang = np.arange(T, dtype=np.float64)[:, None] * inv[None, :]
    cos = np.cos(ang).T.astype(np.float32)
    sin = np.sin(ang).T.astype(np.float32)
    c2 = np.concatenate([cos, cos], 0)
    s2 = np.concatenate([sin, sin], 0)
    return np.ascontiguousarray(c2), np.ascontiguousarray(s2)


def build_ffn0_attn(have_partials=True):
    ctx = Ctx("f0a")
    xT = ctx.inp("xT", (E, T))
    if have_partials:
        pA = ctx.inp("pA", (E, T))
        pB = ctx.inp("pB", (E, T))
    wqa = ctx.inp("wqa", (E, 512))
    wka = ctx.inp("wka", (E, 512))
    wva = ctx.inp("wva", (E, 512))
    cosT = ctx.inp("cosT", (128, T))
    sinT = ctx.inp("sinT", (128, T))
    mcausal_in = ctx.inp("mcausal", (128, 128), BF16)
    xcur = ctx.outp("xcur", (E, T))
    attnT = [ctx.outp(f"attnT{j}", (128, T), BF16) for j in range(4)]

    g = emit_consts(ctx)
    g.feed["mcausal"] = np.triu(np.ones((128, 128), np.float32)).astype(_bf16)

    pa = ctx.ps("pa", (128, 512))
    pb = ctx.ps("pb", (128, 512))
    pc = ctx.ps("pc", (128, 512))
    pd = ctx.ps("pd", (128, 512))
    pe = ctx.ps("pe", (128, 512))
    p2 = ctx.ps("p2", (128, 1024))
    pbf = ctx.ps("pbf", (128, 256), BF16)

    xln = [ctx.sb(f"xln{c}", (128, T), BF16) for c in range(8)]
    AT = [ctx.sb(f"AT{j}", (128, T), BF16) for j in range(4)]
    QT = ctx.sb("QT", (128, T), BF16)
    KT = ctx.sb("KT", (128, T), BF16)
    Vh = ctx.sb("Vh", (128, 32 * 128), BF16)
    cosb = ctx.sb("cosb", (128, T), BF16)
    sinb = ctx.sb("sinb", (128, T), BF16)
    rt1 = ctx.sb("rt1", (128, T), BF16)
    rt2 = ctx.sb("rt2", (128, T), BF16)
    mcz = ctx.sb("mcz", (128, 128), BF16)
    wqt = ctx.sb("wqt", (128, 1024), BF16)
    wkt = ctx.sb("wkt", (128, 1024), BF16)
    wvt = ctx.sb("wvt", (128, 1024), BF16)
    xf = [ctx.sb("xf0", (128, T // 2))]
    pf = [ctx.sb("pf0", (128, T // 2))]
    em = ctx.sb("em", (128, 128), BF16)
    Eb = [ctx.sb(f"Eb{i}", (128, 1024), BF16) for i in range(2)]
    asb = ctx.sb("asb", (128, 256))
    rl_f = ctx.sb("rl_f", (1, 256))
    rl_b = ctx.sb("rl_b", (1, 256), BF16)

    ctx.ld(mcz[:], mcausal_in, mcz)
    ctx.ld(cosb[:], cosT, cosb, eng="gpsimd")
    ctx.ld(sinb[:], sinT, sinb, eng="gpsimd")
    onesP = ctx.sb("onesP", (65, 512), BF16)
    ctx.op("vector", lambda e: e.memset(onesP[:], 1.0), writes=[onesP])

    H2 = T // 2
    for c in range(8):
        for hf in range(2):
            xb_ = xf[0]
            rsl = slice(c * 128, c * 128 + 128)
            csl = slice(hf * H2, hf * H2 + H2)
            ctx.ld(xb_[:], xT[rsl, csl], xb_, eng="sync")
            if have_partials:
                pb_ = pf[0]
                ctx.ld(pb_[:], pA[rsl, csl], pb_, eng="gpsimd")
                ctx.op("vector", lambda e, xb_=xb_, pb_=pb_: e.tensor_tensor(
                    out=xb_[:], in0=xb_[:], in1=pb_[:], op=ALU.add),
                    reads=[pb_], writes=[xb_])
                ctx.ld(pb_[:], pB[rsl, csl], pb_, eng="gpsimd")
                ctx.op("vector", lambda e, xb_=xb_, pb_=pb_: e.tensor_tensor(
                    out=xb_[:], in0=xb_[:], in1=pb_[:], op=ALU.add),
                    reads=[pb_], writes=[xb_])
            ctx.st(xcur[rsl, csl], xb_[:], xb_, eng="sync")
            ctx.op("vector", lambda e, c=c, csl=csl, xb_=xb_: e.tensor_copy(
                out=xln[c][:, csl], in_=xb_[:]), reads=[xb_], writes=[xln[c]])
    emit_ln(ctx, g, xln, T, dict(pa=pa, pb=pb, pc=pc, pd=pd, pe=pe, p2=p2,
                                 pbf=pbf, rows=ctx.sb("rows5", (65, T), BF16),
                                 onesP=onesP))

    started = {}

    def accmm(out_ap, lhsT, rhs, in_ts):
        bank = out_ap.tensor
        reg = (int(out_ap.offset) * mybir.dt.size(out_ap.dtype)) // 2048
        key = (id(bank), out_ap.base_partition(), reg)
        st = not started.get(key, False)
        started[key] = True
        return ctx.mm(out_ap, lhsT, rhs, bank, in_ts, start=st, stop=False)

    def reset(bank):
        for k in [k for k in started if k[0] == id(bank)]:
            started[k] = False

    SC = 1.0 / np.sqrt(128.0)
    for j in range(4):
        hs = slice(128 * j, 128 * j + 128)
        ctx.ld(wqt[:], wqa.rearrange("(c p) f -> p c f", p=128)[:, :, hs], wqt, eng="gpsimd")
        ctx.ld(wkt[:], wka.rearrange("(c p) f -> p c f", p=128)[:, :, hs], wkt, eng="gpsimd")
        ctx.ld(wvt[:], wva.rearrange("(c p) f -> p c f", p=128)[:, :, hs], wvt, eng="gpsimd")
        for (wt, dst) in ((wqt, QT), (wkt, KT)):
            for nt in range(8):
                tsl = slice(nt * 512, nt * 512 + 512)
                reset(pe)
                for cc in range(8):
                    accmm(pe[:, :], wt[:, cc * 128:(cc + 1) * 128], xln[cc][:, tsl],
                          [wt, xln[cc]])
                eng = "vector" if nt % 2 == 0 else "scalar"
                if eng == "vector":
                    ctx.op(eng, lambda e, dst=dst, tsl=tsl: e.tensor_copy(
                        out=dst[:, tsl], in_=pe[:, :]), reads=[pe], writes=[dst])
                else:
                    ctx.op(eng, lambda e, dst=dst, tsl=tsl: e.copy(
                        out=dst[:, tsl], in_=pe[:, :]), reads=[pe], writes=[dst])
        for dst in (QT, KT):
            ctx.sbcp(rt1[0:64, :], dst[64:128, :], rt1, dst, eng="sync")
            ctx.sbcp(rt1[64:128, :], dst[0:64, :], rt1, dst, eng="sync")
            ctx.op("vector", lambda e, dst=dst: e.tensor_tensor(
                out=rt2[0:64, :], in0=rt1[0:64, :], in1=sinb[0:64, :], op=ALU.mult),
                reads=[rt1, sinb], writes=[rt2])
            ctx.op("vector", lambda e, dst=dst: e.tensor_tensor(
                out=rt2[64:128, :], in0=rt1[64:128, :], in1=sinb[64:128, :], op=ALU.mult),
                reads=[rt1, sinb], writes=[rt2])
            ctx.op("vector", lambda e, dst=dst: e.tensor_tensor(
                out=dst[0:64, :], in0=dst[0:64, :], in1=cosb[0:64, :], op=ALU.mult),
                reads=[cosb], writes=[dst])
            ctx.op("vector", lambda e, dst=dst: e.tensor_tensor(
                out=dst[0:64, :], in0=dst[0:64, :], in1=rt2[0:64, :], op=ALU.subtract),
                reads=[rt2], writes=[dst])
            ctx.op("vector", lambda e, dst=dst: e.tensor_tensor(
                out=dst[64:128, :], in0=dst[64:128, :], in1=cosb[64:128, :], op=ALU.mult),
                reads=[cosb], writes=[dst])
            ctx.op("vector", lambda e, dst=dst: e.tensor_tensor(
                out=dst[64:128, :], in0=dst[64:128, :], in1=rt2[64:128, :], op=ALU.add),
                reads=[rt2], writes=[dst])
        for tc in range(32):
            tsl = slice(tc * 128, tc * 128 + 128)
            reset(pd)
            for cc in range(8):
                accmm(pd[:, 0:128], xln[cc][:, tsl], wvt[:, cc * 128:(cc + 1) * 128],
                      [xln[cc], wvt])
            ctx.op("vector", lambda e, tc=tc: e.tensor_copy(
                out=Vh[:, tc * 128:tc * 128 + 128], in_=pd[:, 0:128]),
                reads=[pd], writes=[Vh])
        for q in range(16):
            reset(pa)
            reset(pb)
            nch = 2 * q + 2
            groups = [list(range(gg, min(gg + 4, nch))) for gg in range(0, nch, 4)]
            for chunks in groups:
                reset(p2)
                Ecur = Eb[(chunks[0] // 4) % 2]
                for ci, c in enumerate(chunks):
                    csl = slice(ci * 256, ci * 256 + 256)
                    accmm(p2[:, csl], KT[:, c * 128:(c + 1) * 128],
                          QT[:, q * 256:q * 256 + 256], [KT, QT])
                ctx.op("scalar", lambda e, Ecur=Ecur, n=len(chunks): e.activation(
                    out=Ecur[:, 0:n * 256], in_=p2[:, 0:n * 256], func=AF.Exp,
                    bias=g.zc[:, 0:1], scale=SC), reads=[p2, g.zc], writes=[Ecur])
                for ci, c in enumerate(chunks):
                    vsl = Vh[:, c * 128:c * 128 + 128]
                    if c < 2 * q:
                        esl = Ecur[:, ci * 256:ci * 256 + 256]
                        accmm(pa[:, 0:256], vsl, esl, [Vh, Ecur])
                        accmm(pb[0:1, 0:256], g.onesc[:], esl, [g.onesc, Ecur])
                        continue
                    for s in (0, 1):
                        delta = c - (2 * q + s)
                        if delta > 0:
                            continue
                        esl = Ecur[:, ci * 256 + s * 128:ci * 256 + s * 128 + 128]
                        if delta == 0:
                            ctx.op("vector", lambda e, esl=esl: e.tensor_tensor(
                                out=em[:], in0=esl, in1=mcz[:], op=ALU.mult),
                                reads=[Ecur, mcz], writes=[em])
                            esl = em[:]
                            ints = [em]
                        else:
                            ints = [Ecur]
                        accmm(pa[:, s * 128:s * 128 + 128], vsl, esl, [Vh] + ints)
                        accmm(pb[0:1, s * 128:s * 128 + 128], g.onesc[:], esl,
                              [g.onesc] + ints)
            qsl = slice(q * 256, q * 256 + 256)
            ctx.op("vector", lambda e: e.tensor_copy(out=asb[:], in_=pa[:, 0:256]),
                   reads=[pa], writes=[asb])
            ctx.op("vector", lambda e: e.reciprocal(out=rl_f[:], in_=pb[0:1, 0:256]),
                   reads=[pb], writes=[rl_f])
            ctx.op("vector", lambda e: e.tensor_copy(out=rl_b[:], in_=rl_f[:]),
                   reads=[rl_f], writes=[rl_b])
            reset(pe)
            accmm(pe[:, 0:256], g.onesr[0:1, 0:128], rl_b[0:1, :], [g.onesr, rl_b])
            ctx.op("vector", lambda e, j=j, qsl=qsl: e.tensor_tensor(
                out=AT[j][:, qsl], in0=asb[:], in1=pe[:, 0:256], op=ALU.mult),
                reads=[asb, pe], writes=[AT[j]])
    for j in range(4):
        ctx.st(attnT[j], AT[j][:], AT[j], eng="sync")
    return ctx, g


def run_ffn0_attn(x_parts, w_in, key="f0a"):
    have_p = len(x_parts) == 3
    ck = (key, have_p)
    if ck not in _A0_CACHE:
        ctx, g = build_ffn0_attn(have_partials=have_p)
        nc = ctx.finish()
        _A0_CACHE[ck] = (nc, g)
    nc, g = _A0_CACHE[ck]
    cosT, sinT = rope_tables()
    w_inT = np.ascontiguousarray(w_in.T)
    in_maps = []
    for c in range(NCORES):
        b, hh = c // 2, c % 2
        m = dict(g.feed)
        m["xT"] = x_parts[0][b]
        if have_p:
            m["pA"] = x_parts[1][b]
            m["pB"] = x_parts[2][b]
        m["wqa"] = np.ascontiguousarray(w_inT[:, E + 512 * hh:E + 512 * hh + 512])
        m["wka"] = np.ascontiguousarray(w_inT[:, 2 * E + 512 * hh:2 * E + 512 * hh + 512])
        m["wva"] = np.ascontiguousarray(w_inT[:, 3 * E + 512 * hh:3 * E + 512 * hh + 512])
        m["cosT"] = cosT
        m["sinT"] = sinT
        in_maps.append(m)
    res = run_bass_kernel_spmd(nc, in_maps, core_ids=list(range(NCORES)))
    xc = [np.asarray(res.results[2 * b]["xcur"], np.float32) for b in range(4)]
    at = [np.concatenate([np.asarray(res.results[2 * b + hh][f"attnT{j}"])
                          for hh in (0, 1) for j in range(4)], 0)
          for b in range(4)]
    return xc, at


def build_ffn0_exp():
    TC2 = T // 2
    ctx = Ctx("f0b")
    xcurT = ctx.inp("xcurT", (E, TC2))
    attnTs = ctx.inp("attnTs", (E, TC2), BF16)
    wqf = ctx.inp("wqf", (E, E))
    wrt = ctx.inp("wrt", (E, 16))
    keT = ctx.inp("keT", (16, 128, 2048))
    veT = ctx.inp("veT", (16, 128, 2048))
    w_outT = ctx.inp("w_outT", (2 * E, E))
    xnew = ctx.outp("xnew", (E, TC2))

    g = emit_consts(ctx)
    pa = ctx.ps("pa", (128, 512))
    pb = ctx.ps("pb", (128, 512))
    pc = ctx.ps("pc", (128, 512))
    pd = ctx.ps("pd", (128, 512))
    pe = ctx.ps("pe", (128, 512))
    p2 = ctx.ps("p2", (128, 1024))
    pbf = ctx.ps("pbf", (128, 256), BF16)

    xln = [ctx.sb(f"xln{c}", (128, TC2), BF16) for c in range(8)]
    qfT = [ctx.sb(f"qfT{c}", (128, TC2), BF16) for c in range(8)]
    acc = [ctx.sb(f"acc{c}", (128, TC2)) for c in range(8)]
    ke = [ctx.sb(f"ke{i}", (128, 2048), BF16) for i in range(2)]
    ve = [ctx.sb(f"ve{i}", (128, 2048), BF16) for i in range(2)]
    wq_t = ctx.sb("wq_t", (128, 1024), BF16)
    wrt_t = ctx.sb("wrt_t", (128, 128), BF16)
    wot = [ctx.sb(f"wot{i}", (128, 2048), BF16) for i in range(2)]
    rlT = ctx.sb("rlT", (16, TC2), BF16)
    rltm = ctx.sb("rltm", (128, 16))
    m1c = ctx.sb("m1c", (128, 1))
    m2c = ctx.sb("m2c", (128, 1))
    s1c = ctx.sb("s1c", (128, 1))
    s2c = ctx.sb("s2c", (128, 1))
    msk1 = ctx.sb("msk1", (128, 16))
    msk2 = ctx.sb("msk2", (128, 16))
    rl2 = ctx.sb("rl2", (128, 16))
    gtm = ctx.sb("gtm", (128, 16))
    gtmb = ctx.sb("gtmb", (128, 16), BF16)
    Gsb = ctx.sb("Gsb", (16, TC2), BF16)
    gb = ctx.sb("gb", (128, 512), BF16)
    aT = ctx.sb("aT", (128, 512), BF16)
    tmp = ctx.sb("tmp", (128, 512))
    onesP = ctx.sb("onesP", (65, 512), BF16)
    xfr = [ctx.sb(f"xfr{i}", (128, 512)) for i in range(2)]
    onew = [ctx.sb("onew0", (128, 512))]
    ctx.op("vector", lambda e: e.memset(onesP[:], 1.0), writes=[onesP])

    atb = xln
    for c in range(8):
        ctx.ld(xln[c][:], xcurT[c * 128:(c + 1) * 128, :], xln[c], eng="gpsimd")
    rows5 = ctx.sb("rows5", (65, TC2), BF16)
    mu_tm, r_tm = emit_ln(ctx, g, xln, TC2,
                          dict(pa=pa, pb=pb, pc=pc, pd=pd, pe=pe, p2=p2,
                               pbf=pbf, rows=rows5, onesP=onesP))
    ctx.ld(wrt_t[:], wrt.rearrange("(c p) f -> p c f", p=128), wrt_t, eng="gpsimd")

    started = {}

    def accmm(out_ap, lhsT, rhs, in_ts):
        bank = out_ap.tensor
        reg = (int(out_ap.offset) * mybir.dt.size(out_ap.dtype)) // 2048
        key = (id(bank), out_ap.base_partition(), reg)
        st = not started.get(key, False)
        started[key] = True
        return ctx.mm(out_ap, lhsT, rhs, bank, in_ts, start=st, stop=False)

    def reset(bank):
        for k in [k for k in started if k[0] == id(bank)]:
            started[k] = False

    NT2 = TC2 // 512
    for hj in range(8):
        ctx.ld(wq_t[:], wqf.rearrange("(c p) f -> p c f", p=128)[:, :, 128 * hj:128 * hj + 128],
               wq_t, eng="gpsimd")
        for nt in range(NT2):
            tsl = slice(nt * 512, nt * 512 + 512)
            reset(pe)
            for cc in range(8):
                accmm(pe[:, :], wq_t[:, cc * 128:(cc + 1) * 128], xln[cc][:, tsl],
                      [wq_t, xln[cc]])
            eng = "vector" if nt % 2 == 0 else "scalar"
            if eng == "vector":
                ctx.op(eng, lambda e, hj=hj, tsl=tsl: e.tensor_copy(
                    out=qfT[hj][:, tsl], in_=pe[:, :]), reads=[pe], writes=[qfT[hj]])
            else:
                ctx.op(eng, lambda e, hj=hj, tsl=tsl: e.copy(
                    out=qfT[hj][:, tsl], in_=pe[:, :]), reads=[pe], writes=[qfT[hj]])
    wrtf = ctx.sb("wrtf", (128, 128))
    wlo_t = ctx.sb("wlo_t", (128, 128), BF16)
    xhi = ctx.sb("xhi", (128, 512), BF16)
    xlo = ctx.sb("xlo", (128, 512), BF16)
    xfl = ctx.sb("xfl", (128, 512))
    ncwb = ctx.sb("ncwb", (128, 16))
    rlP = acc[0]
    ctx.ld(wrtf[:], wrt.rearrange("(c p) f -> p c f", p=128), wrtf, eng="sync")
    ctx.op("vector", lambda e: e.tensor_copy(out=wrt_t[:], in_=wrtf[:]),
           reads=[wrtf], writes=[wrt_t])
    ctx.op("vector", lambda e: e.tensor_tensor(
        out=xfl[:, 0:128], in0=wrtf[:], in1=wrt_t[:], op=ALU.subtract),
        reads=[wrtf, wrt_t], writes=[xfl])
    ctx.op("vector", lambda e: e.tensor_copy(out=wlo_t[:], in_=xfl[:, 0:128]),
           reads=[xfl], writes=[wlo_t])
    for nt in range(NT2):
        tsl = slice(nt * 512, nt * 512 + 512)
        reset(pd)
        for cc in range(8):
            ctx.ld(xfl[:], xcurT[cc * 128:(cc + 1) * 128, tsl], xfl, eng="sync")
            ctx.op("vector", lambda e: e.tensor_copy(out=xhi[:], in_=xfl[:]),
                   reads=[xfl], writes=[xhi])
            ctx.op("vector", lambda e: e.tensor_tensor(
                out=xfl[:], in0=xfl[:], in1=xhi[:], op=ALU.subtract),
                reads=[xhi], writes=[xfl])
            ctx.op("vector", lambda e: e.tensor_copy(out=xlo[:], in_=xfl[:]),
                   reads=[xfl], writes=[xlo])
            accmm(pd[0:16, :], wrt_t[:, cc * 16:(cc + 1) * 16], xhi[:], [wrt_t, xhi])
            accmm(pd[0:16, :], wrt_t[:, cc * 16:(cc + 1) * 16], xlo[:], [wrt_t, xlo])
            accmm(pd[0:16, :], wlo_t[:, cc * 16:(cc + 1) * 16], xhi[:], [wlo_t, xhi])
        ctx.op("vector", lambda e, tsl=tsl: e.tensor_copy(
            out=rlP[0:16, tsl], in_=pd[0:16, :]), reads=[pd], writes=[rlP])
    reset(pe)
    for cc in range(8):
        accmm(pe[0:1, 0:16], g.onesc[:], wrt_t[:, cc * 16:(cc + 1) * 16],
              [g.onesc, wrt_t])
    ctx.op("vector", lambda e: e.tensor_copy(out=rlT[0:1, 0:16], in_=pe[0:1, 0:16]),
           reads=[pe], writes=[rlT])
    reset(pe)
    accmm(pe[:, 0:16], g.onesr[0:1, 0:128], rlT[0:1, 0:16], [g.onesr, rlT])
    ctx.op("vector", lambda e: e.tensor_scalar_mul(
        out=ncwb[:], in0=pe[:, 0:16], scalar1=-1.0), reads=[pe], writes=[ncwb])
    nts2 = TC2 // 128
    for tt in range(TC2 // 128):
        tsl = slice(tt * 128, tt * 128 + 128)
        ctx.op("tensor", lambda e, tsl=tsl: e.transpose(
            pd[0:128, 0:16], rlP[0:16, tsl], g.idenf[0:16, 0:16]),
            reads=[rlP, g.idenf], writes=[pd])
        ctx.op("vector", lambda e, tt=tt: e.scalar_tensor_tensor(
            out=rltm[:], in0=ncwb[:], scalar=mu_tm[:, tt:tt + 1], in1=pd[0:128, 0:16],
            op0=ALU.mult, op1=ALU.add), reads=[ncwb, mu_tm, pd], writes=[rltm])
        ctx.op("vector", lambda e, tt=tt: e.tensor_scalar(
            out=rltm[:], in0=rltm[:], scalar1=r_tm[:, tt:tt + 1], scalar2=None,
            op0=ALU.mult), reads=[rltm, r_tm], writes=[rltm])
        ctx.op("vector", lambda e: e.reduce_max(out=m1c[:], in_=rltm[:], axis=AX.X),
               reads=[rltm], writes=[m1c])
        ctx.op("vector", lambda e: e.tensor_scalar(
            out=msk1[:], in0=rltm[:], scalar1=m1c[:, 0:1], scalar2=None,
            op0=ALU.is_equal), reads=[rltm, m1c], writes=[msk1])
        ctx.op("vector", lambda e: e.scalar_tensor_tensor(
            out=rl2[:], in0=msk1[:], scalar=-1e30, in1=rltm[:],
            op0=ALU.mult, op1=ALU.add), reads=[msk1, rltm], writes=[rl2])
        ctx.op("vector", lambda e: e.reduce_max(out=m2c[:], in_=rl2[:], axis=AX.X),
               reads=[rl2], writes=[m2c])
        ctx.op("vector", lambda e: e.tensor_scalar(
            out=msk2[:], in0=rl2[:], scalar1=m2c[:, 0:1], scalar2=None,
            op0=ALU.is_equal), reads=[rl2, m2c], writes=[msk2])
        ctx.op("scalar", lambda e: e.activation(
            out=s1c[:], in_=m1c[:], func=AF.Sigmoid, bias=g.zc[:, 0:1], scale=1.0),
            reads=[m1c, g.zc], writes=[s1c])
        ctx.op("scalar", lambda e: e.activation(
            out=s2c[:], in_=m2c[:], func=AF.Sigmoid, bias=g.zc[:, 0:1], scale=1.0),
            reads=[m2c, g.zc], writes=[s2c])
        ctx.op("vector", lambda e: e.tensor_scalar(
            out=gtm[:], in0=msk1[:], scalar1=s1c[:, 0:1], scalar2=None,
            op0=ALU.mult), reads=[msk1, s1c], writes=[gtm])
        ctx.op("vector", lambda e: e.scalar_tensor_tensor(
            out=gtm[:], in0=msk2[:], scalar=s2c[:, 0:1], in1=gtm[:],
            op0=ALU.mult, op1=ALU.add), reads=[msk2, s2c], writes=[gtm])
        ctx.op("vector", lambda e: e.tensor_copy(out=gtmb[:], in_=gtm[:]),
               reads=[gtm], writes=[gtmb])
        ctx.op("tensor", lambda e: e.transpose(
            pbf[0:16, 0:128], gtmb[:, 0:16], g.iden[:]), reads=[gtmb, g.iden],
            writes=[pbf])
        ctx.op("vector", lambda e, tsl=tsl: e.tensor_copy(
            out=Gsb[:, tsl], in_=pbf[0:16, 0:128]), reads=[pbf], writes=[Gsb])
    for ei in range(16):
        kb = ke[ei % 2]
        vb = ve[ei % 2]
        ctx.ld(kb[:], keT[ei], kb, eng="gpsimd")
        ctx.ld(vb[:], veT[ei], vb, eng="gpsimd")
        Grow = rows5
        ctx.sbcp(Grow[0:1, :], Gsb[ei:ei + 1, :], Grow, Gsb, eng="gpsimd")
        for nt in range(NT2):
            tsl = slice(nt * 512, nt * 512 + 512)
            reset(pe)
            accmm(pe[:, :], g.onesr[0:1, 0:128], Grow[0:1, tsl], [g.onesr, Grow])
            ctx.op("vector", lambda e: e.tensor_copy(out=gb[:], in_=pe[:, :]),
                   reads=[pe], writes=[gb])
            for hj in range(8):
                reset(pc)
                for sc in range(2):
                    reset(pa)
                    accmm(pa[:, :], kb[:, hj * 256 + sc * 128:hj * 256 + sc * 128 + 128],
                          qfT[hj][:, tsl], [kb, qfT[hj]])
                    ctx.op("scalar", lambda e: e.activation(
                        out=aT[:], in_=pa[:, :], func=AF.Gelu, bias=g.zc[:, 0:1],
                        scale=1.0), reads=[pa, g.zc], writes=[aT])
                    accmm(pc[:, :], vb[:, (hj * 2 + sc) * 128:(hj * 2 + sc) * 128 + 128],
                          aT[:], [vb, aT])
                ctx.op("vector", lambda e: e.tensor_tensor(
                    out=tmp[:], in0=pc[:, :], in1=gb[:], op=ALU.mult),
                    reads=[pc, gb], writes=[tmp])
                if ei == 0:
                    ctx.op("vector", lambda e, hj=hj, tsl=tsl: e.tensor_copy(
                        out=acc[hj][:, tsl], in_=tmp[:]), reads=[tmp], writes=[acc[hj]])
                else:
                    ctx.op("vector", lambda e, hj=hj, tsl=tsl: e.tensor_tensor(
                        out=acc[hj][:, tsl], in0=acc[hj][:, tsl], in1=tmp[:],
                        op=ALU.add), reads=[tmp], writes=[acc[hj]])
    for c in range(8):
        ctx.ld(atb[c][:], attnTs[c * 128:(c + 1) * 128, :], atb[c], eng="sync")
    for c in range(8):
        ctx.op("vector", lambda e, c=c: e.tensor_copy(out=qfT[c][:], in_=acc[c][:]),
               reads=[acc[c]], writes=[qfT[c]])
    for fc in range(8):
        fsl = slice(fc * 128, fc * 128 + 128)
        ctx.ld(wot[fc % 2][:], w_outT.rearrange("(c p) f -> c p f", p=128)[:, :, fsl]
               .rearrange("c p f -> p (c f)") if False else
               w_outT.rearrange("(c p) f -> p c f", p=128)[:, :, fsl],
               wot[fc % 2], eng="gpsimd")
        for nt in range(NT2):
            tsl = slice(nt * 512, nt * 512 + 512)
            bank = pa if nt % 2 == 0 else pb
            reset(bank)
            for cc in range(16):
                rhs = atb[cc][:, tsl] if cc < 8 else qfT[cc - 8][:, tsl]
                rt = atb[cc] if cc < 8 else qfT[cc - 8]
                accmm(bank[:, :], wot[fc % 2][:, cc * 128:(cc + 1) * 128], rhs,
                      [wot[fc % 2], rt])
            xr = xfr[nt % 2]
            ctx.ld(xr[:], xcurT[fsl, tsl], xr, eng="sync")
            ob = onew[0]
            ctx.op("vector", lambda e, bank=bank, ob=ob, xr=xr: e.tensor_tensor(
                out=ob[:], in0=bank[:, :], in1=xr[:], op=ALU.add),
                reads=[bank, xr], writes=[ob])
            ctx.st(xnew[fsl, tsl], ob[:], ob, eng="sync")
    return ctx, g


def run_ffn0_exp(xcur_list, attnT_list, w_in, k_ffwd, v_ffwd, w_out):
    if "f0b" not in _A0_CACHE:
        ctx, g = build_ffn0_exp()
        nc = ctx.finish()
        _A0_CACHE["f0b"] = (nc, g)
    nc, g = _A0_CACHE["f0b"]
    w_inT = np.ascontiguousarray(w_in.T)
    keT = np.ascontiguousarray(k_ffwd.transpose(0, 3, 1, 2).reshape(16, 128, 2048))
    veT = np.ascontiguousarray(
        v_ffwd.reshape(16, 8, 2, 128, 128).transpose(0, 3, 1, 2, 4).reshape(16, 128, 2048))
    w_outT = np.ascontiguousarray(w_out.T)
    TC2 = T // 2
    in_maps = []
    for c in range(NCORES):
        b, th = c // 2, c % 2
        m = dict(g.feed)
        csl = slice(th * TC2, th * TC2 + TC2)
        m["xcurT"] = np.ascontiguousarray(xcur_list[b][:, csl])
        m["attnTs"] = np.ascontiguousarray(attnT_list[b][:, csl])
        m["wqf"] = np.ascontiguousarray(w_inT[:, 0:E])
        m["wrt"] = np.ascontiguousarray(w_inT[:, 4 * E:4 * E + 16])
        m["keT"] = keT
        m["veT"] = veT
        m["w_outT"] = w_outT
        in_maps.append(m)
    res = run_bass_kernel_spmd(nc, in_maps, core_ids=list(range(NCORES)))
    return [np.concatenate([np.asarray(res.results[2 * b]["xnew"], np.float32),
                            np.asarray(res.results[2 * b + 1]["xnew"], np.float32)], 1)
            for b in range(4)]


def mla_tables():
    half = 16
    inv = 1.0 / 10000.0 ** (np.arange(half, dtype=np.float64) / half)
    ang = np.arange(T, dtype=np.float64)[:, None] * inv[None, :]
    cos = np.cos(ang).T.astype(np.float32)
    sin = np.sin(ang).T.astype(np.float32)
    cz = np.zeros((64, T), np.float32)
    sz = np.zeros((64, T), np.float32)
    cz[0:16] = cos
    cz[32:48] = cos
    sz[0:16] = sin
    sz[32:48] = sin
    return np.ascontiguousarray(cz), np.ascontiguousarray(sz)


def mla_madd():
    out = {}
    for o in range(4):
        i = o * 128 + np.arange(128)[:, None]
        j = np.arange(512)[None, :]
        out[o] = np.where(j <= i, 0.0, -1e30).astype(np.float32)
    return out


def build_mla():
    ctx = Ctx("mla")
    xT = ctx.inp("xT", (E, T))
    wq = ctx.inp("wq", (E, 512))
    wqr = ctx.inp("wqr", (E, 256))
    wkr = ctx.inp("wkr", (E, 32))
    wkvd = ctx.inp("wkvd", (E, 256))
    wku = ctx.inp("wku", (256, 512))
    wvu = ctx.inp("wvu", (256, 512))
    woR = ctx.inp("woR", (512, E))
    lobo = ctx.inp("lobo", (1, 8))
    crT = ctx.inp("crT", (64, T))
    srT = ctx.inp("srT", (64, T))
    part = ctx.outp("part", (E, T))

    g = emit_consts(ctx)
    for o in range(4):
        ap = ctx.inp(f"madd{o}", (128, 512), BF16)
        t = ctx.sb(f"g_madd{o}", (128, 512), BF16)
        ctx.ld(t[:], ap, t)
        setattr(g, f"madd{o}", t)
    g.feed.update({f"madd{o}": mla_madd()[o].astype(_bf16) for o in range(4)})
    mcz_in = ctx.inp("mcausal", (128, 128), BF16)
    mcz = ctx.sb("mcz", (128, 128), BF16)
    ctx.ld(mcz[:], mcz_in, mcz)
    g.feed["mcausal"] = np.triu(np.ones((128, 128), np.float32)).astype(_bf16)

    pa = ctx.ps("pa", (128, 512))
    pb = ctx.ps("pb", (128, 512))
    pc = ctx.ps("pc", (128, 512))
    pd = ctx.ps("pd", (128, 512))
    pe = ctx.ps("pe", (128, 512))
    p2 = ctx.ps("p2", (128, 1024))
    pbf = ctx.ps("pbf", (128, 256), BF16)

    xln = [ctx.sb(f"xln{c}", (128, T), BF16) for c in range(8)]
    YT = [ctx.sb(f"YT{i}", (128, T), BF16) for i in range(4)]
    kv = [ctx.sb(f"kv{i}", (128, T), BF16) for i in range(2)]
    krT = ctx.sb("krT", (64, T), BF16)
    qrT = ctx.sb("qrT", (64, T), BF16)
    rr1 = ctx.sb("rr1", (64, T // 2), BF16)
    rr2 = ctx.sb("rr2", (64, T // 2), BF16)
    cosb = ctx.sb("cosb", (64, T), BF16)
    sinb = ctx.sb("sinb", (64, T), BF16)
    QQ = ctx.sb("QQ", (97, T), BF16)
    KK = ctx.sb("KK", (97, T), BF16)
    Vh = ctx.sb("Vh", (128, 32 * 65), BF16)
    st65 = ctx.sb("st65", (65, T), BF16)
    onesP = ctx.sb("onesP", (65, 512), BF16)
    wqt = ctx.sb("wqt", (128, 512), BF16)
    wq2 = ctx.sb("wq2", (128, 256), BF16)
    wk2 = ctx.sb("wk2", (128, 256), BF16)
    wkv_t = ctx.sb("wkv_t", (128, 2048), BF16)
    wku_t = ctx.sb("wku_t", (128, 128), BF16)
    wvu_t = ctx.sb("wvu_t", (128, 128), BF16)
    wot = [ctx.sb(f"wot{i}", (128, 1024), BF16) for i in range(4)]
    cvals = ctx.sb("cvals", (1, 8))
    Eb = [ctx.sb(f"Eb{i}", (128, 512), BF16) for i in range(2)]
    em = ctx.sb("em", (128, 128), BF16)
    mred = ctx.sb("mred", (128, 2))
    mcol = ctx.sb("mcol", (128, 2))
    mcb = ctx.sb("mcb", (128, 2), BF16)
    afin = ctx.sb("afin", (65, 256))
    rl_b = ctx.sb("rl_b", (65, 256), BF16)

    obuf = [ctx.sb("obuf0", (128, 512))]
    mtmp = obuf[0]
    rl_f = obuf[0]
    ctx.op("vector", lambda e: e.memset(onesP[:], 1.0), writes=[onesP])
    ctx.op("vector", lambda e: e.memset(st65[32:33, :], 1.0), writes=[st65])
    ctx.op("vector", lambda e: e.memset(Vh[:, 64:32 * 65:65], 1.0), writes=[Vh])
    ctx.ld(cosb[:], crT, cosb, eng="gpsimd")
    ctx.ld(sinb[:], srT, sinb, eng="gpsimd")
    ctx.ld(cvals[:], lobo, cvals, eng="sync")
    ctx.op("scalar", lambda e: e.activation(out=cvals[:], in_=cvals[:], func=AF.Exp,
                                            bias=g.zc[0:1, 0:1], scale=1.0),
           reads=[g.zc], writes=[cvals])
    ctx.ld(wkv_t[:], wkvd.rearrange("(c p) f -> p c f", p=128), wkv_t, eng="gpsimd")
    ctx.ld(wk2[:], wkr.rearrange("(c p) f -> p c f", p=128)[:, :, :].rearrange(
        "p c f -> p (c f)") if False else wkr.rearrange("(c p) f -> p c f", p=128),
        wk2, eng="gpsimd")
    for cc in range(4):
        ctx.ld(wot[cc][:], woR.rearrange("(c p) f -> c p f", p=128)[cc],
               wot[cc], eng="gpsimd")

    for c in range(8):
        ctx.ld(xln[c][:], xT[c * 128:(c + 1) * 128, :], xln[c], eng="gpsimd")
    emit_ln(ctx, g, xln, T, dict(pa=pa, pb=pb, pc=pc, pd=pd, pe=pe, p2=p2,
                                 pbf=pbf, rows=st65, onesP=onesP,
                                 dtile=obuf[0][0:128, 0:128],
                                 t1=obuf[0][0:128, 128:160],
                                 t2=obuf[0][0:128, 160:192],
                                 statS=obuf[0][0:128, 192:256],
                                 P=Eb[0][0:128, 0:64],
                                 Ts=Eb[0][0:64, 64:192]))

    started = {}

    def accmm(out_ap, lhsT, rhs, in_ts):
        bank = out_ap.tensor
        reg = (int(out_ap.offset) * mybir.dt.size(out_ap.dtype)) // 2048
        key = (id(bank), out_ap.base_partition(), reg)
        st = not started.get(key, False)
        started[key] = True
        return ctx.mm(out_ap, lhsT, rhs, bank, in_ts, start=st, stop=False)

    def reset(bank):
        for k in [k for k in started if k[0] == id(bank)]:
            started[k] = False

    def rope32(dst):
        H2 = T // 2
        for hf in range(2):
            cs = slice(hf * H2, hf * H2 + H2)
            ctx.sbcp(rr1[0:16, :], dst[32:48, cs], rr1, dst, eng="sync")
            ctx.sbcp(rr1[32:48, :], dst[0:16, cs], rr1, dst, eng="sync")
            for r0 in (0, 32):
                ctx.op("vector", lambda e, r0=r0, cs=cs: e.tensor_tensor(
                    out=rr2[r0:r0 + 16, :], in0=rr1[r0:r0 + 16, :],
                    in1=sinb[r0:r0 + 16, cs], op=ALU.mult),
                    reads=[rr1, sinb], writes=[rr2])
                ctx.op("vector", lambda e, r0=r0, cs=cs: e.tensor_tensor(
                    out=dst[r0:r0 + 16, cs], in0=dst[r0:r0 + 16, cs],
                    in1=cosb[r0:r0 + 16, cs], op=ALU.mult),
                    reads=[cosb], writes=[dst])
            ctx.op("vector", lambda e, cs=cs: e.tensor_tensor(
                out=dst[0:16, cs], in0=dst[0:16, cs], in1=rr2[0:16, :],
                op=ALU.subtract), reads=[rr2], writes=[dst])
            ctx.op("vector", lambda e, cs=cs: e.tensor_tensor(
                out=dst[32:48, cs], in0=dst[32:48, cs], in1=rr2[32:48, :],
                op=ALU.add), reads=[rr2], writes=[dst])

    for half_ in range(2):
        for nt in range(8):
            tsl = slice(nt * 512, nt * 512 + 512)
            reset(pe)
            for cc in range(8):
                accmm(pe[:, :], wkv_t[:, cc * 2 * 128 + half_ * 128:
                                      cc * 2 * 128 + half_ * 128 + 128]
                      if False else
                      wkv_t[:, (cc * 2 + half_) * 128:(cc * 2 + half_) * 128 + 128],
                      xln[cc][:, tsl], [wkv_t, xln[cc]])
            ctx.op("vector", lambda e, half_=half_, tsl=tsl: e.tensor_copy(
                out=kv[half_][:, tsl], in_=pe[:, :]), reads=[pe], writes=[kv[half_]])
    for nt in range(8):
        tsl = slice(nt * 512, nt * 512 + 512)
        reset(pd)
        for cc in range(8):
            accmm(pd[0:16, :], wk2[:, cc * 32:cc * 32 + 16], xln[cc][:, tsl],
                  [wk2, xln[cc]])
            accmm(pd[32:48, :], wk2[:, cc * 32 + 16:cc * 32 + 32], xln[cc][:, tsl],
                  [wk2, xln[cc]])
        ctx.op("vector", lambda e, tsl=tsl: e.tensor_copy(
            out=krT[0:16, tsl], in_=pd[0:16, :]), reads=[pd], writes=[krT])
        ctx.op("vector", lambda e, tsl=tsl: e.tensor_copy(
            out=krT[32:48, tsl], in_=pd[32:48, :]), reads=[pd], writes=[krT])
    rope32(krT)
    ctx.sbcp(KK[64:80, :], krT[0:16, :], KK, krT, eng="sync")
    ctx.sbcp(KK[80:96, :], krT[32:48, :], KK, krT, eng="sync")
    ctx.sbcp(KK[96:97, :], st65[32:33, :], KK, st65, eng="sync")

    SCM = 1.0 / np.sqrt(96.0)
    for h in range(8):
        hs64 = slice(64 * h, 64 * h + 64)
        ctx.ld(wqt[:], wq.rearrange("(c p) f -> p c f", p=128)[:, :, hs64], wqt,
               eng="gpsimd")
        ctx.ld(wq2[:], wqr.rearrange("(c p) f -> p c f", p=128)[:, :, 32 * h:32 * h + 32],
               wq2, eng="gpsimd")
        ctx.ld(wku_t[:], wku.rearrange("(c p) f -> p c f", p=128)[:, :, hs64],
               wku_t, eng="gpsimd")
        ctx.ld(wvu_t[:], wvu.rearrange("(c p) f -> p c f", p=128)[:, :, hs64],
               wvu_t, eng="gpsimd")
        for nt in range(8):
            tsl = slice(nt * 512, nt * 512 + 512)
            reset(pe)
            for cc in range(8):
                accmm(pe[0:64, :], wqt[:, cc * 64:(cc + 1) * 64], xln[cc][:, tsl],
                      [wqt, xln[cc]])
            ctx.op("vector", lambda e, tsl=tsl: e.tensor_copy(
                out=QQ[0:64, tsl], in_=pe[0:64, :]), reads=[pe], writes=[QQ])
            reset(pd)
            for cc in range(8):
                accmm(pd[0:16, :], wq2[:, cc * 32:cc * 32 + 16], xln[cc][:, tsl],
                      [wq2, xln[cc]])
                accmm(pd[32:48, :], wq2[:, cc * 32 + 16:cc * 32 + 32], xln[cc][:, tsl],
                      [wq2, xln[cc]])
            ctx.op("vector", lambda e, tsl=tsl: e.tensor_copy(
                out=qrT[0:16, tsl], in_=pd[0:16, :]), reads=[pd], writes=[qrT])
            ctx.op("vector", lambda e, tsl=tsl: e.tensor_copy(
                out=qrT[32:48, tsl], in_=pd[32:48, :]), reads=[pd], writes=[qrT])
            reset(pc)
            for kc in range(2):
                accmm(pc[0:64, :], wku_t[:, kc * 64:(kc + 1) * 64], kv[kc][:, tsl],
                      [wku_t, kv[kc]])
            ctx.op("vector", lambda e, tsl=tsl: e.tensor_copy(
                out=KK[0:64, tsl], in_=pc[0:64, :]), reads=[pc], writes=[KK])
        rope32(qrT)
        ctx.sbcp(QQ[64:80, :], qrT[0:16, :], QQ, qrT, eng="sync")
        ctx.sbcp(QQ[80:96, :], qrT[32:48, :], QQ, qrT, eng="sync")
        for tc in range(32):
            tsl = slice(tc * 128, tc * 128 + 128)
            reset(pd)
            for kc in range(2):
                accmm(pd[:, 0:64], kv[kc][:, tsl], wvu_t[:, kc * 64:(kc + 1) * 64],
                      [kv[kc], wvu_t])
            ctx.op("vector", lambda e, tc=tc: e.tensor_copy(
                out=Vh[:, tc * 65:tc * 65 + 64], in_=pd[:, 0:64]),
                reads=[pd], writes=[Vh])
        for it in range(32):
            isl = slice(it * 128, it * 128 + 128)
            nch = it // 4 + 1
            first = True
            for ch in range(nch):
                ksl = slice(ch * 512, ch * 512 + 512)
                bank = pa if ch % 2 == 0 else pb
                reset(bank)
                accmm(bank[:, :], QQ[0:96, isl], KK[0:96, ksl], [QQ, KK])
                if ch == it // 4:
                    mk = getattr(g, f"madd{it % 4}")
                    ctx.op("vector", lambda e, bank=bank, mk=mk: e.tensor_tensor(
                        out=mtmp[:], in0=bank[:, :], in1=mk[:], op=ALU.add),
                        reads=[bank, mk], writes=[mtmp])
                    src_t, src_ap = mtmp, mtmp[:]
                else:
                    src_t, src_ap = bank, bank[:, :]
                ctx.op("vector", lambda e, src_ap=src_ap: e.reduce_max(
                    out=mred[:, 0:1], in_=src_ap, axis=AX.X),
                    reads=[src_t], writes=[mred])
                if first:
                    ctx.op("vector", lambda e: e.tensor_copy(
                        out=mcol[:, 0:1], in_=mred[:, 0:1]), reads=[mred], writes=[mcol])
                    first = False
                else:
                    ctx.op("vector", lambda e: e.tensor_tensor(
                        out=mcol[:, 0:1], in0=mcol[:, 0:1], in1=mred[:, 0:1],
                        op=ALU.max), reads=[mred], writes=[mcol])
            ctx.op("vector", lambda e: e.tensor_scalar_mul(
                out=mcol[:, 1:2], in0=mcol[:, 0:1], scalar1=-1.0),
                reads=[mcol], writes=[mcol])
            ctx.op("vector", lambda e: e.tensor_copy(out=mcb[:, 0:1], in_=mcol[:, 1:2]),
                   reads=[mcol], writes=[mcb])
            ctx.op("tensor", lambda e: e.transpose(pbf[0:1, 0:128], mcb[:, 0:1],
                                                   g.iden[:]),
                   reads=[mcb, g.iden], writes=[pbf])
            ctx.op("vector", lambda e, isl=isl: e.tensor_copy(
                out=st65[0:1, isl], in_=pbf[0:1, 0:128]), reads=[pbf], writes=[st65])
        ctx.sbcp(QQ[96:97, :], st65[0:1, :], QQ, st65, eng="sync")
        for q in range(16):
            reset(pa)
            nch = 2 * q + 2
            groups = [list(range(gg, min(gg + 2, nch))) for gg in range(0, nch, 2)]
            for chunks in groups:
                reset(p2)
                Ecur = Eb[(chunks[0] // 2) % 2]
                for ci, c in enumerate(chunks):
                    csl = slice(ci * 256, ci * 256 + 256)
                    accmm(p2[:, csl], KK[:, c * 128:(c + 1) * 128],
                          QQ[:, q * 256:q * 256 + 256], [KK, QQ])
                ctx.op("scalar", lambda e, Ecur=Ecur, n=len(chunks): e.activation(
                    out=Ecur[:, 0:n * 256], in_=p2[:, 0:n * 256], func=AF.Exp,
                    bias=g.zc[:, 0:1], scale=SCM), reads=[p2, g.zc], writes=[Ecur])
                for ci, c in enumerate(chunks):
                    vsl = Vh[:, c * 65:c * 65 + 65]
                    if c < 2 * q:
                        esl = Ecur[:, ci * 256:ci * 256 + 256]
                        accmm(pa[0:65, 0:256], vsl, esl, [Vh, Ecur])
                        continue
                    for s in (0, 1):
                        delta = c - (2 * q + s)
                        if delta > 0:
                            continue
                        esl = Ecur[:, ci * 256 + s * 128:ci * 256 + s * 128 + 128]
                        ints = [Ecur]
                        if delta == 0:
                            ctx.op("vector", lambda e, esl=esl: e.tensor_tensor(
                                out=em[:], in0=esl, in1=mcz[:], op=ALU.mult),
                                reads=[Ecur, mcz], writes=[em])
                            esl = em[:]
                            ints = [em]
                        accmm(pa[0:65, s * 128:s * 128 + 128], vsl, esl, [Vh] + ints)
            qsl = slice(q * 256, q * 256 + 256)
            ctx.op("vector", lambda e: e.tensor_copy(out=afin[:, :], in_=pa[0:65, 0:256]),
                   reads=[pa], writes=[afin])
            ctx.op("vector", lambda e, h=h: e.tensor_scalar(
                out=rl_f[64:65, 0:256], in0=afin[64:65, :], scalar1=cvals[0:1, h:h + 1],
                scalar2=None, op0=ALU.add), reads=[afin, cvals], writes=[rl_f])
            ctx.op("vector", lambda e: e.reciprocal(
                out=rl_f[64:65, 0:256], in_=rl_f[64:65, 0:256]), reads=[rl_f], writes=[rl_f])
            ctx.op("vector", lambda e: e.tensor_copy(
                out=rl_b[64:65, :], in_=rl_f[64:65, 0:256]), reads=[rl_f], writes=[rl_b])
            reset(pe)
            accmm(pe[0:64, 256:512], onesP[64:65, 0:64], rl_b[64:65, :], [onesP, rl_b])
            yt = YT[h // 2]
            r0 = 64 * (h % 2)
            ctx.op("vector", lambda e, yt=yt, r0=r0, qsl=qsl: e.tensor_tensor(
                out=yt[r0:r0 + 64, qsl], in0=afin[0:64, :], in1=pe[0:64, 256:512],
                op=ALU.mult), reads=[afin, pe], writes=[yt])
    for fc in range(8):
        fsl = slice(fc * 128, fc * 128 + 128)
        for nt in range(8):
            tsl = slice(nt * 512, nt * 512 + 512)
            bank = pa if nt % 2 == 0 else pb
            reset(bank)
            for cc in range(4):
                accmm(bank[:, :], wot[cc][:, fsl], YT[cc][:, tsl], [wot[cc], YT[cc]])
            ob = obuf[0]
            ctx.op("vector", lambda e, bank=bank, ob=ob: e.tensor_copy(
                out=ob[:], in_=bank[:]), reads=[bank], writes=[ob])
            ctx.st(part[fsl, tsl], ob[:], ob, eng="sync")
    return ctx, g


def run_mla(x, Wq, Wqr, Wkr, Wkvd, Wku, Wvu, Wo, lobo_log):
    if "mla" not in _A0_CACHE:
        ctx, g = build_mla()
        nc = ctx.finish()
        _A0_CACHE["mla"] = (nc, g)
    nc, g = _A0_CACHE["mla"]
    crT, srT = mla_tables()
    in_maps = []
    for c in range(NCORES):
        b, hh = c // 2, c % 2
        m = dict(g.feed)
        m["xT"] = (x[b] if isinstance(x, list)
                   else np.ascontiguousarray(x[b].T) if x.ndim == 3 else x[b])
        m["wq"] = np.ascontiguousarray(Wq[:, 512 * hh:512 * hh + 512])
        m["wqr"] = np.ascontiguousarray(Wqr[:, 256 * hh:256 * hh + 256])
        m["wkr"] = np.ascontiguousarray(Wkr)
        m["wkvd"] = np.ascontiguousarray(Wkvd)
        m["wku"] = np.ascontiguousarray(Wku[:, 512 * hh:512 * hh + 512])
        m["wvu"] = np.ascontiguousarray(Wvu[:, 512 * hh:512 * hh + 512])
        m["woR"] = np.ascontiguousarray(Wo[512 * hh:512 * hh + 512, :])
        m["lobo"] = np.ascontiguousarray(lobo_log.reshape(1, 16)[:, 8 * hh:8 * hh + 8])
        m["crT"] = crT
        m["srT"] = srT
        in_maps.append(m)
    res = run_bass_kernel_spmd(nc, in_maps, core_ids=list(range(NCORES)))
    return [np.asarray(r["part"], np.float32) for r in res.results]


def xpos_tables():
    DH, W = 64, 512
    half_idx = np.arange(0, DH, 2, dtype=np.float64)
    inv = 1.0 / 10000.0 ** (half_idx / DH)
    t = np.arange(T, dtype=np.float64)
    fr = t[:, None] * inv[None, :]
    base = (half_idx + 0.4 * DH) / (1.4 * DH)
    power = (t - T // 2) / float(W // 2)
    scale = base[None, :] ** power[:, None]
    cos, sin = np.cos(fr), np.sin(fr)
    csq = np.concatenate([cos * scale, cos * scale], 1).T * 8.0
    ssq = np.concatenate([sin * scale, sin * scale], 1).T * 8.0
    csk = np.concatenate([cos / scale, cos / scale], 1).T
    ssk = np.concatenate([sin / scale, sin / scale], 1).T
    return tuple(np.ascontiguousarray(a.astype(np.float32))
                 for a in (csq, ssq, csk, ssk))


def build_l1f_attn(first):
    ctx = Ctx("l1a")
    xT = ctx.inp("xT", (E, T))
    if first:
        pA = ctx.inp("pA", (E, T))
        pB = ctx.inp("pB", (E, T))
        xcur = ctx.outp("xcur", (E, T))
        z0out = ctx.outp("z0", (E, T))
    wq = ctx.inp("wq", (E, 256))
    wk = ctx.inp("wk", (E, 256))
    wv = ctx.inp("wv", (E, 256))
    qs_in = ctx.inp("qs", (64, 1))
    ks_in = ctx.inp("ks", (64, 1))
    woR = ctx.inp("woR", (256, E))
    csq = ctx.inp("csq", (64, T))
    ssq = ctx.inp("ssq", (64, T))
    csk = ctx.inp("csk", (64, T))
    ssk = ctx.inp("ssk", (64, T))
    part = ctx.outp("part", (E, T))

    g = emit_consts(ctx)
    mcz_in = ctx.inp("mcausal", (128, 128), BF16)
    mlow_in = ctx.inp("mlow", (128, 128), BF16)
    mcz = ctx.sb("mcz", (128, 128), BF16)
    mlow = ctx.sb("mlz", (128, 128), BF16)
    ctx.ld(mcz[:], mcz_in, mcz)
    ctx.ld(mlow[:], mlow_in, mlow)
    g.feed["mcausal"] = np.triu(np.ones((128, 128), np.float32)).astype(_bf16)
    g.feed["mlow"] = np.tril(np.ones((128, 128), np.float32)).astype(_bf16)

    pa = ctx.ps("pa", (128, 512))
    pb = ctx.ps("pb", (128, 512))
    pc = ctx.ps("pc", (128, 512))
    pd = ctx.ps("pd", (128, 512))
    pe = ctx.ps("pe", (128, 512))
    p2 = ctx.ps("p2", (128, 1024))
    pbf = ctx.ps("pbf", (128, 256), BF16)

    xln = [ctx.sb(f"xln{c}", (128, T), BF16) for c in range(8)]
    YT = [ctx.sb(f"YT{i}", (128, T), BF16) for i in range(2)]
    QT = ctx.sb("QT", (64, T), BF16)
    KT = ctx.sb("KT", (64, T), BF16)
    Vh = ctx.sb("Vh", (128, 32 * 65), BF16)
    csqb = ctx.sb("csqb", (64, T), BF16)
    ssqb = ctx.sb("ssqb", (64, T), BF16)
    cskb = ctx.sb("cskb", (64, T), BF16)
    sskb = ctx.sb("sskb", (64, T), BF16)
    rnb = ctx.sb("rnb", (64, T), BF16)
    rr1 = ctx.sb("rr1", (64, T // 2), BF16)
    rr2 = ctx.sb("rr2", (64, T // 2), BF16)
    sqrow = ctx.sb("sqrow", (65, T))
    st65 = ctx.sb("st65", (65, T), BF16)
    onesP = ctx.sb("onesP", (65, 512), BF16)
    qsc = ctx.sb("qsc", (64, 2))
    wqt = ctx.sb("wqt", (128, 512), BF16)
    wvt = ctx.sb("wvt", (128, 512), BF16)
    wot = [ctx.sb(f"wot{i}", (128, 1024), BF16) for i in range(2)]
    Eb = [ctx.sb(f"Eb{i}", (128, 512), BF16) for i in range(2)]
    em = ctx.sb("em", (128, 128), BF16)
    shv = ctx.sb("shv", (128, 128), BF16)
    afin = ctx.sb("afin", (65, 256))
    rl_b = ctx.sb("rl_b", (65, 256), BF16)
    obuf = [ctx.sb("obuf0", (128, 512))]
    rl_f = obuf[0]
    if first:
        xf = ctx.sb("xf0", (128, T // 2))
        pf = ctx.sb("pf0", (128, T // 2))

    ctx.op("vector", lambda e: e.memset(onesP[:], 1.0), writes=[onesP])
    ctx.op("vector", lambda e: e.memset(Vh[:, 64:32 * 65:65], 1.0), writes=[Vh])
    ctx.ld(csqb[:], csq, csqb, eng="gpsimd")
    ctx.ld(ssqb[:], ssq, ssqb, eng="gpsimd")
    ctx.ld(cskb[:], csk, cskb, eng="gpsimd")
    ctx.ld(sskb[:], ssk, sskb, eng="gpsimd")
    ctx.ld(qsc[:, 0:1], qs_in, qsc, eng="sync")
    ctx.ld(qsc[:, 1:2], ks_in, qsc, eng="sync")

    H2 = T // 2
    if first:
        for c in range(8):
            for hf in range(2):
                rsl = slice(c * 128, c * 128 + 128)
                csl = slice(hf * H2, hf * H2 + H2)
                ctx.ld(xf[:], xT[rsl, csl], xf, eng="sync")
                ctx.ld(pf[:], pA[rsl, csl], pf, eng="gpsimd")
                ctx.op("vector", lambda e: e.tensor_tensor(
                    out=xf[:], in0=xf[:], in1=pf[:], op=ALU.add),
                    reads=[pf], writes=[xf])
                ctx.ld(pf[:], pB[rsl, csl], pf, eng="gpsimd")
                ctx.op("vector", lambda e: e.tensor_tensor(
                    out=xf[:], in0=xf[:], in1=pf[:], op=ALU.add),
                    reads=[pf], writes=[xf])
                ctx.st(xcur[rsl, csl], xf[:], xf, eng="sync")
                ctx.op("vector", lambda e, c=c, csl=csl: e.tensor_copy(
                    out=xln[c][:, csl], in_=xf[:]), reads=[xf], writes=[xln[c]])
        t1, t2 = emit_ln(ctx, g, xln, T,
                         dict(pa=pa, pb=pb, pc=pc, pd=pd, pe=pe, p2=p2,
                              pbf=pbf, rows=st65, onesP=onesP,
                              dtile=obuf[0][0:128, 0:128]))
        for i in range(T // 512):
            sl = slice(i * 512, i * 512 + 512)
            ctx.mm(pd[:], onesP[32:33, 0:128], st65[32:33, sl], pd, [onesP, st65],
                   start=True, stop=True)
            ctx.mm(pe[:], g.onesr[0:1, 0:128], st65[0:1, sl], pe, [g.onesr, st65],
                   start=True, stop=True)
            for c in range(8):
                rsl = slice(c * 128, c * 128 + 128)
                ctx.ld(xf[:, 0:512], xcur[rsl, sl], xf, eng="sync")
                ctx.op("vector", lambda e: e.tensor_tensor(
                    out=xf[:, 0:512], in0=xf[:, 0:512], in1=pd[:], op=ALU.mult),
                    reads=[pd], writes=[xf])
                ctx.op("vector", lambda e: e.tensor_tensor(
                    out=xf[:, 0:512], in0=xf[:, 0:512], in1=pe[:], op=ALU.subtract),
                    reads=[pe], writes=[xf])
                ctx.st(z0out[rsl, sl], xf[:, 0:512], xf, eng="sync")
    else:
        for c in range(8):
            ctx.ld(xln[c][:], xT[c * 128:(c + 1) * 128, :], xln[c], eng="gpsimd")

    started = {}

    def accmm(out_ap, lhsT, rhs, in_ts):
        bank = out_ap.tensor
        reg = (int(out_ap.offset) * mybir.dt.size(out_ap.dtype)) // 2048
        key = (id(bank), out_ap.base_partition(), reg)
        st = not started.get(key, False)
        started[key] = True
        return ctx.mm(out_ap, lhsT, rhs, bank, in_ts, start=st, stop=False)

    def reset(bank):
        for k in [k for k in started if k[0] == id(bank)]:
            started[k] = False

    def shifted_proj(out_psum, wt, tsl, in_ts):
        t0 = tsl.start
        for cc in range(8):
            lhs = wt[:, cc * 64:(cc + 1) * 64]
            if cc < 4:
                accmm(out_psum, lhs, xln[cc][:, tsl], in_ts + [xln[cc]])
            else:
                if t0 == 0:
                    accmm(out_psum.tensor[out_psum.base_partition():
                                          out_psum.base_partition() + 64, 1:512]
                          if False else out_psum, lhs, xln[cc][:, 0:511], in_ts + [xln[cc]])
                else:
                    accmm(out_psum, lhs, xln[cc][:, t0 - 1:t0 + 511], in_ts + [xln[cc]])

    def rope64(dst, cb, sb_):
        for hf in range(2):
            cs = slice(hf * H2, hf * H2 + H2)
            ctx.sbcp(rr1[0:32, :], dst[32:64, cs], rr1, dst, eng="sync")
            ctx.sbcp(rr1[32:64, :], dst[0:32, cs], rr1, dst, eng="sync")
            for r0 in (0, 32):
                ctx.op("vector", lambda e, r0=r0, cs=cs: e.tensor_tensor(
                    out=rr2[r0:r0 + 32, :], in0=rr1[r0:r0 + 32, :],
                    in1=sb_[r0:r0 + 32, cs], op=ALU.mult),
                    reads=[rr1, sb_], writes=[rr2])
                ctx.op("vector", lambda e, r0=r0, cs=cs: e.tensor_tensor(
                    out=dst[r0:r0 + 32, cs], in0=dst[r0:r0 + 32, cs],
                    in1=cb[r0:r0 + 32, cs], op=ALU.mult),
                    reads=[cb], writes=[dst])
            ctx.op("vector", lambda e, cs=cs: e.tensor_tensor(
                out=dst[0:32, cs], in0=dst[0:32, cs], in1=rr2[0:32, :],
                op=ALU.subtract), reads=[rr2], writes=[dst])
            ctx.op("vector", lambda e, cs=cs: e.tensor_tensor(
                out=dst[32:64, cs], in0=dst[32:64, cs], in1=rr2[32:64, :],
                op=ALU.add), reads=[rr2], writes=[dst])

    for j in range(4):
        hs = slice(64 * j, 64 * j + 64)
        ctx.ld(wqt[:], wq.rearrange("(c p) f -> p c f", p=128)[:, :, hs], wqt, eng="gpsimd")
        for (wi, dst) in ((wq, QT), (wk, KT)):
            if wi is wk:
                ctx.ld(wqt[:], wk.rearrange("(c p) f -> p c f", p=128)[:, :, hs],
                       wqt, eng="gpsimd")
            for nt in range(8):
                tsl = slice(nt * 512, nt * 512 + 512)
                reset(pe)
                if nt == 0:
                    for cc in range(4):
                        accmm(pe[0:64, :], wqt[:, cc * 64:(cc + 1) * 64],
                              xln[cc][:, 0:512], [wqt, xln[cc]])
                    for cc in range(4, 8):
                        accmm(pe[0:64, 1:512], wqt[:, cc * 64:(cc + 1) * 64],
                              xln[cc][:, 0:511], [wqt, xln[cc]])
                else:
                    shifted_proj(pe[0:64, :], wqt, tsl, [wqt])
                ctx.op("vector", lambda e, dst=dst, tsl=tsl: e.tensor_copy(
                    out=dst[:, tsl], in_=pe[0:64, :]), reads=[pe], writes=[dst])
        for (dst, scol, cb, sb_) in ((QT, 0, csqb, ssqb), (KT, 1, cskb, sskb)):
            ctx.op("scalar", lambda e, dst=dst: e.activation(
                out=rnb[:], in_=dst[:], func=AF.Square, bias=g.zc[0:64, 0:1],
                scale=1.0), reads=[dst, g.zc], writes=[rnb])
            for nt in range(8):
                tsl = slice(nt * 512, nt * 512 + 512)
                reset(pd)
                accmm(pd[0:1, :], g.onesc[0:64, :], rnb[0:64, tsl], [g.onesc, rnb])
                ctx.op("vector", lambda e, tsl=tsl: e.tensor_copy(
                    out=sqrow[0:1, tsl], in_=pd[0:1, :]), reads=[pd], writes=[sqrow])
            ctx.op("scalar", lambda e: e.activation(
                out=sqrow[0:1, :], in_=sqrow[0:1, :], func=AF.Sqrt,
                bias=g.zc[0:1, 0:1], scale=1.0), reads=[g.zc], writes=[sqrow])
            ctx.op("vector", lambda e: e.tensor_scalar_max(
                out=sqrow[0:1, :], in0=sqrow[0:1, :], scalar1=1e-12),
                reads=[sqrow], writes=[sqrow])
            ctx.op("vector", lambda e: e.reciprocal(
                out=sqrow[0:1, :], in_=sqrow[0:1, :]), reads=[sqrow], writes=[sqrow])
            ctx.op("vector", lambda e: e.tensor_copy(
                out=st65[0:1, :], in_=sqrow[0:1, :]), reads=[sqrow], writes=[st65])
            for nt in range(8):
                tsl = slice(nt * 512, nt * 512 + 512)
                reset(pe)
                accmm(pe[0:64, :], g.onesr[0:1, 0:64], st65[0:1, tsl], [g.onesr, st65])
                ctx.op("vector", lambda e, tsl=tsl: e.tensor_copy(
                    out=rnb[0:64, tsl], in_=pe[0:64, :]), reads=[pe], writes=[rnb])
            ctx.op("vector", lambda e, dst=dst, scol=scol: e.scalar_tensor_tensor(
                out=dst[:], in0=dst[:], scalar=qsc[:, scol:scol + 1], in1=rnb[0:64, :],
                op0=ALU.mult, op1=ALU.mult), reads=[qsc, rnb], writes=[dst])
            rope64(dst, cb, sb_)
        ctx.ld(wvt[:], wv.rearrange("(c p) f -> p c f", p=128)[:, :, hs], wvt, eng="gpsimd")
        for tc in range(32):
            tsl = slice(tc * 128, tc * 128 + 128)
            reset(pd)
            for cc in range(8):
                if cc < 4:
                    accmm(pd[:, 0:64], xln[cc][:, tsl], wvt[:, cc * 64:(cc + 1) * 64],
                          [xln[cc], wvt])
                else:
                    if tc == 0:
                        ctx.op("vector", lambda e: e.memset(shv[:, 0:1], 0.0),
                               writes=[shv])
                        ctx.op("vector", lambda e, cc=cc: e.tensor_copy(
                            out=shv[:, 1:128], in_=xln[cc][:, 0:127]),
                            reads=[xln[cc]], writes=[shv])
                        accmm(pd[:, 0:64], shv[:], wvt[:, cc * 64:(cc + 1) * 64],
                              [shv, wvt])
                    else:
                        accmm(pd[:, 0:64], xln[cc][:, tc * 128 - 1:tc * 128 + 127],
                              wvt[:, cc * 64:(cc + 1) * 64], [xln[cc], wvt])
            ctx.op("vector", lambda e, tc=tc: e.tensor_copy(
                out=Vh[:, tc * 65:tc * 65 + 64], in_=pd[:, 0:64]),
                reads=[pd], writes=[Vh])
        for q in range(16):
            reset(pa)
            for c in range(max(0, 2 * q - 4), 2 * q + 2):
                reset(p2)
                Ecur = Eb[c % 2]
                accmm(p2[:, 0:256], KT[:, c * 128:(c + 1) * 128],
                      QT[:, q * 256:q * 256 + 256], [KT, QT])
                ctx.op("scalar", lambda e, Ecur=Ecur: e.activation(
                    out=Ecur[:, 0:256], in_=p2[:, 0:256], func=AF.Exp,
                    bias=g.zc[:, 0:1], scale=1.0), reads=[p2, g.zc], writes=[Ecur])
                vsl = Vh[:, c * 65:c * 65 + 65]
                for s in (0, 1):
                    delta = c - (2 * q + s)
                    if delta > 0 or delta < -4:
                        continue
                    esl = Ecur[:, s * 128:s * 128 + 128]
                    ints = [Ecur]
                    if delta == 0:
                        ctx.op("vector", lambda e, esl=esl: e.tensor_tensor(
                            out=em[:], in0=esl, in1=mcz[:], op=ALU.mult),
                            reads=[Ecur, mcz], writes=[em])
                        esl, ints = em[:], [em]
                    elif delta == -4:
                        ctx.op("vector", lambda e, esl=esl: e.tensor_tensor(
                            out=em[:], in0=esl, in1=mlow[:], op=ALU.mult),
                            reads=[Ecur, mlow], writes=[em])
                        esl, ints = em[:], [em]
                    accmm(pa[0:65, s * 128:s * 128 + 128], vsl, esl, [Vh] + ints)
            qsl = slice(q * 256, q * 256 + 256)
            ctx.op("vector", lambda e: e.tensor_copy(out=afin[:, :], in_=pa[0:65, 0:256]),
                   reads=[pa], writes=[afin])
            ctx.op("vector", lambda e: e.reciprocal(
                out=rl_f[64:65, 0:256], in_=afin[64:65, :]), reads=[afin], writes=[rl_f])
            ctx.op("vector", lambda e: e.tensor_copy(
                out=rl_b[64:65, :], in_=rl_f[64:65, 0:256]), reads=[rl_f], writes=[rl_b])
            reset(pe)
            accmm(pe[0:64, 256:512], onesP[64:65, 0:64], rl_b[64:65, :], [onesP, rl_b])
            yt = YT[j // 2]
            r0 = 64 * (j % 2)
            ctx.op("vector", lambda e, yt=yt, r0=r0, qsl=qsl: e.tensor_tensor(
                out=yt[r0:r0 + 64, qsl], in0=afin[0:64, :], in1=pe[0:64, 256:512],
                op=ALU.mult), reads=[afin, pe], writes=[yt])
    for cc in range(2):
        ctx.ld(wot[cc][:], woR.rearrange("(c p) f -> c p f", p=128)[cc],
               wot[cc], eng="gpsimd")
    for fc in range(8):
        fsl = slice(fc * 128, fc * 128 + 128)
        for nt in range(8):
            tsl = slice(nt * 512, nt * 512 + 512)
            bank = pa if nt % 2 == 0 else pb
            reset(bank)
            for cc in range(2):
                accmm(bank[:, :], wot[cc][:, fsl], YT[cc][:, tsl], [wot[cc], YT[cc]])
            ob = obuf[0]
            ctx.op("vector", lambda e, bank=bank, ob=ob: e.tensor_copy(
                out=ob[:], in_=bank[:]), reads=[bank], writes=[ob])
            ctx.st(part[fsl, tsl], ob[:], ob, eng="sync")
    return ctx, g


L1F_BLOCKS = [(jb * 128, 128 if jb < 21 else 42) for jb in range(22)]


def l1f_perm():
    acols, gcols, rows = [], [], []
    wcols = []
    for off, sz in L1F_BLOCKS:
        wcols.extend(range(off, off + sz))
        wcols.extend(range(2730 + off, 2730 + off + sz))
        rows.extend(range(off, off + sz))
    return np.array(wcols), np.array(rows)


def build_l1f_ffn(last):
    TC2 = T // 2
    TH = TC2 + 1
    ctx = Ctx("l1f")
    z0s = ctx.inp("z0s", (E, TH))
    pAs = ctx.inp("pAs", (E, TH))
    pBs = ctx.inp("pBs", (E, TH))
    lg = ctx.inp("lg", (128, 8))
    lb = ctx.inp("lb", (128, 8))
    w1p = ctx.inp("w1p", (E, 5460))
    b1p = ctx.inp("b1p", (128, 44))
    w2p = ctx.inp("w2p", (2730, E))
    b2in = ctx.inp("b2in", (128, 8))
    if last:
        xcs = ctx.inp("xcs", (E, TC2))
    zout = ctx.outp("zout", (E, TC2))

    g = emit_consts(ctx)
    pa = ctx.ps("pa", (128, 512))
    pb = ctx.ps("pb", (128, 512))
    pc = ctx.ps("pc", (128, 512))
    pd = ctx.ps("pd", (128, 512))
    pe = ctx.ps("pe", (128, 512))
    p2 = ctx.ps("p2", (128, 1024))
    pbf = ctx.ps("pbf", (128, 256), BF16)

    z1t = [ctx.sb(f"z1t{c}", (128, TH)) for c in range(8)]
    hs = [ctx.sb(f"hs{c}", (128, TC2), BF16) for c in range(8)]
    h2 = ctx.sb("h2", (128, 22 * 512), BF16)
    wa = [ctx.sb(f"wa{i}", (128, 1024), BF16) for i in range(2)]
    wg = [ctx.sb(f"wg{i}", (128, 1024), BF16) for i in range(2)]
    w2t = [ctx.sb(f"w2t{i}", (128, 512), BF16) for i in range(2)]
    lgc = ctx.sb("lgc", (128, 8))
    lbc = ctx.sb("lbc", (128, 8))
    b1c = ctx.sb("b1c", (128, 44))
    b2c = ctx.sb("b2c", (128, 8))
    ab = ctx.sb("ab", (128, 512), BF16)
    gb = ctx.sb("gb", (128, 512), BF16)
    onesP = ctx.sb("onesP", (65, 512), BF16)
    st65 = ctx.sb("st65", (65, TC2), BF16)
    pf = ctx.sb("pf", (128, TH))
    xr = ctx.sb("xr", (128, 512))
    ob = ctx.sb("ob", (128, 512))
    ctx.op("vector", lambda e: e.memset(onesP[:], 1.0), writes=[onesP])
    ctx.ld(lgc[:], lg, lgc, eng="sync")
    ctx.ld(lbc[:], lb, lbc, eng="sync")
    ctx.ld(b1c[:], b1p, b1c, eng="sync")
    ctx.ld(b2c[:], b2in, b2c, eng="sync")

    for c in range(8):
        ctx.ld(z1t[c][:], z0s[c * 128:(c + 1) * 128, :], z1t[c], eng="sync")
        ctx.ld(pf[:], pAs[c * 128:(c + 1) * 128, :], pf, eng="gpsimd")
        ctx.op("vector", lambda e, c=c: e.tensor_tensor(
            out=z1t[c][:], in0=z1t[c][:], in1=pf[:], op=ALU.add),
            reads=[pf], writes=[z1t[c]])
        ctx.ld(pf[:], pBs[c * 128:(c + 1) * 128, :], pf, eng="gpsimd")
        ctx.op("vector", lambda e, c=c: e.tensor_tensor(
            out=z1t[c][:], in0=z1t[c][:], in1=pf[:], op=ALU.add),
            reads=[pf], writes=[z1t[c]])
        sh = slice(1, TH) if c < 4 else slice(0, TC2)
        ctx.op("vector", lambda e, c=c, sh=sh: e.tensor_copy(
            out=hs[c][:], in_=z1t[c][:, sh]), reads=[z1t[c]], writes=[hs[c]])
    emit_ln(ctx, g, hs, TC2, dict(pa=pa, pb=pb, pc=pc, pd=pd, pe=pe, p2=p2,
                                  pbf=pbf, rows=st65, onesP=onesP,
                                  affine=(lgc, lbc)))

    started = {}

    def accmm(out_ap, lhsT, rhs, in_ts):
        bank = out_ap.tensor
        reg = (int(out_ap.offset) * mybir.dt.size(out_ap.dtype)) // 2048
        key = (id(bank), out_ap.base_partition(), reg)
        st = not started.get(key, False)
        started[key] = True
        return ctx.mm(out_ap, lhsT, rhs, bank, in_ts, start=st, stop=False)

    def reset(bank):
        for k in [k for k in started if k[0] == id(bank)]:
            started[k] = False

    pco = 0
    for tb in range(4):
        tsl = slice(tb * 512, tb * 512 + 512)
        for jb, (off, sz) in enumerate(L1F_BLOCKS):
            awt = wa[jb % 2]
            gwt = wg[jb % 2]
            pcol = 2 * (off + (0 if jb < 21 else 0))
            c0 = 2 * off
            ctx.ld(awt[:, 0:8 * sz],
                   w1p.rearrange("(c p) f -> p c f", p=128)[:, :, c0:c0 + sz],
                   awt, eng="gpsimd")
            ctx.ld(gwt[:, 0:8 * sz],
                   w1p.rearrange("(c p) f -> p c f", p=128)[:, :, c0 + sz:c0 + 2 * sz],
                   gwt, eng="gpsimd")
            bank_a = pa if jb % 2 == 0 else pc
            bank_g = pb if jb % 2 == 0 else pd
            reset(bank_a)
            reset(bank_g)
            for cc in range(8):
                accmm(bank_a[0:sz, :], awt[:, cc * sz:(cc + 1) * sz], hs[cc][:, tsl],
                      [awt, hs[cc]])
                accmm(bank_g[0:sz, :], gwt[:, cc * sz:(cc + 1) * sz], hs[cc][:, tsl],
                      [gwt, hs[cc]])
            ctx.op("vector", lambda e, bank_a=bank_a, jb=jb, sz=sz: e.tensor_scalar(
                out=ab[0:sz, :], in0=bank_a[0:sz, :],
                scalar1=b1c[0:sz, 2 * jb:2 * jb + 1],
                scalar2=None, op0=ALU.add), reads=[bank_a, b1c], writes=[ab])
            ctx.op("scalar", lambda e, bank_g=bank_g, jb=jb, sz=sz: e.activation(
                out=gb[0:sz, :], in_=bank_g[0:sz, :], func=AF.Gelu,
                bias=b1c[0:sz, 2 * jb + 1:2 * jb + 2], scale=1.0),
                reads=[bank_g, b1c], writes=[gb])
            ctx.op("vector", lambda e, jb=jb, sz=sz: e.tensor_tensor(
                out=h2[0:sz, jb * 512:jb * 512 + 512], in0=ab[0:sz, :],
                in1=gb[0:sz, :], op=ALU.mult), reads=[ab, gb], writes=[h2])
        for wave in range(2):
            banks = [pa, pb, pc, pd]
            for bank in banks:
                reset(bank)
            for jb, (off, sz) in enumerate(L1F_BLOCKS):
                wt = w2t[jb % 2]
                ctx.ld(wt[0:sz, :], w2p[off:off + sz, wave * 512:wave * 512 + 512],
                       wt, eng="gpsimd")
                for f4 in range(4):
                    accmm(banks[f4][:, :], wt[0:sz, f4 * 128:f4 * 128 + 128],
                          h2[0:sz, jb * 512:jb * 512 + 512], [wt, h2])
            for f4 in range(4):
                fc = wave * 4 + f4
                fsl = slice(fc * 128, fc * 128 + 128)
                ctx.op("vector", lambda e, f4=f4, fc=fc, tsl=tsl: e.tensor_scalar(
                    out=ob[:], in0=banks[f4][:, :], scalar1=b2c[:, fc:fc + 1],
                    scalar2=None, op0=ALU.add), reads=[banks[f4], b2c], writes=[ob])
                ctx.op("vector", lambda e, fc=fc, tb=tb: e.tensor_tensor(
                    out=ob[:], in0=ob[:],
                    in1=z1t[fc][:, 1 + tb * 512:1 + tb * 512 + 512], op=ALU.add),
                    reads=[z1t[fc]], writes=[ob])
                if last:
                    ctx.ld(xr[:], xcs[fsl, tsl], xr, eng="sync")
                    ctx.op("vector", lambda e: e.tensor_tensor(
                        out=ob[:], in0=ob[:], in1=xr[:], op=ALU.add),
                        reads=[xr], writes=[ob])
                ctx.st(zout[fsl, tsl], ob[:], ob, eng="sync")
    return ctx, g


def _halo(a, th):
    TC2 = T // 2
    if th == 0:
        return np.ascontiguousarray(
            np.concatenate([np.zeros((E, 1), np.float32), a[:, 0:TC2]], 1))
    return np.ascontiguousarray(a[:, TC2 - 1:2 * TC2])


def run_l1f(x_or_z, partials, wqkv_l, qs_l, ks_l, wo_l, lng_l, lnb_l, w1_l,
            b1_l, w2_l, b2_l, first, last_pair, xcur_in=None):
    key = ("l1a", first)
    if key not in _A0_CACHE:
        ctx, g = build_l1f_attn(first)
        _A0_CACHE[key] = (ctx.finish(), g)
    nc, g = _A0_CACHE[key]
    csq, ssq, csk, ssk = xpos_tables()
    wperm, rperm = l1f_perm()
    in_maps = []
    for c in range(NCORES):
        b, hh = c // 2, c % 2
        m = dict(g.feed)
        m["xT"] = x_or_z[b]
        if first:
            m["pA"], m["pB"] = partials[0][b], partials[1][b]
        m["wq"] = np.ascontiguousarray(wqkv_l[:, 256 * hh:256 * hh + 256])
        m["wk"] = np.ascontiguousarray(wqkv_l[:, 512 + 256 * hh:512 + 256 * hh + 256])
        m["wv"] = np.ascontiguousarray(wqkv_l[:, 1024 + 256 * hh:1024 + 256 * hh + 256])
        m["qs"] = np.ascontiguousarray(qs_l.reshape(64, 1))
        m["ks"] = np.ascontiguousarray(ks_l.reshape(64, 1))
        m["woR"] = np.ascontiguousarray(wo_l[256 * hh:256 * hh + 256, :])
        m["csq"], m["ssq"], m["csk"], m["ssk"] = csq, ssq, csk, ssk
        in_maps.append(m)
    res = run_bass_kernel_spmd(nc, in_maps, core_ids=list(range(NCORES)))
    if first:
        xcur = [np.asarray(res.results[2 * b]["xcur"], np.float32) for b in range(4)]
        z0 = [np.asarray(res.results[2 * b]["z0"], np.float32) for b in range(4)]
    else:
        xcur, z0 = None, x_or_z
    pA = [np.asarray(res.results[2 * b]["part"], np.float32) for b in range(4)]
    pB = [np.asarray(res.results[2 * b + 1]["part"], np.float32) for b in range(4)]

    key2 = ("l1f", last_pair)
    if key2 not in _A0_CACHE:
        ctx, g2 = build_l1f_ffn(last_pair)
        _A0_CACHE[key2] = (ctx.finish(), g2)
    nc2, g2 = _A0_CACHE[key2]
    w1pm = np.ascontiguousarray(w1_l[:, wperm])
    b1pm = np.zeros((44, 128), np.float32)
    for jb, (off, sz) in enumerate(L1F_BLOCKS):
        b1pm[2 * jb, 0:sz] = b1_l[off:off + sz]
        b1pm[2 * jb + 1, 0:sz] = b1_l[2730 + off:2730 + off + sz]
    w2pm = np.ascontiguousarray(w2_l[rperm, :])
    in_maps = []
    for c in range(NCORES):
        b, th = c // 2, c % 2
        m = dict(g2.feed)
        m["z0s"] = _halo(z0[b], th)
        m["pAs"] = _halo(pA[b], th)
        m["pBs"] = _halo(pB[b], th)
        m["lg"] = np.ascontiguousarray(lng_l.reshape(8, 128).T)
        m["lb"] = np.ascontiguousarray(lnb_l.reshape(8, 128).T)
        m["w1p"] = w1pm
        m["b1p"] = np.ascontiguousarray(b1pm.T)
        m["w2p"] = w2pm
        m["b2in"] = np.ascontiguousarray(b2_l.reshape(8, 128).T)
        if last_pair:
            xc_src = xcur if xcur is not None else xcur_in
            m["xcs"] = np.ascontiguousarray(
                xc_src[b][:, th * (T // 2):(th + 1) * (T // 2)])
        in_maps.append(m)
    res2 = run_bass_kernel_spmd(nc2, in_maps, core_ids=list(range(NCORES)))
    z2 = [np.concatenate([np.asarray(res2.results[2 * b]["zout"], np.float32),
                          np.asarray(res2.results[2 * b + 1]["zout"], np.float32)], 1)
          for b in range(4)]
    return z2, xcur


def build_delta_scan(debug0=False):
    ctx = Ctx("d1")
    if debug0:
        dbg = {n: ctx.outp("dbg_" + n, sh, BF16) for (n, sh) in (
            ("At", (32, 32)), ("AtT", (32, 32)), ("UWb", (32, 512)),
            ("ui", (32, 256)), ("attnT", (32, 32)), ("ot", (32, 256)),
            ("hq", (256, 32)), ("hk", (256, 32)), ("hv", (256, 32)),
            ("hkb", (256, 32)), ("UW0", (32, 512)), ("UW1", (32, 512)),
            ("Pw2", (32, 32)), ("Pw16T", (32, 32)))}
    xT = ctx.inp("xT", (E, T))
    wq = ctx.inp("wq", (E, 512))
    wk = ctx.inp("wk", (E, 512))
    wv = ctx.inp("wv", (E, 512))
    wb = ctx.inp("wb", (E, 2))
    cwq = ctx.inp("cwq", (128, 16))
    cwk = ctx.inp("cwk", (128, 16))
    cwv = ctx.inp("cwv", (128, 16))
    deltao = ctx.outp("deltao", (2, T, 256), BF16)

    g = emit_consts(ctx)
    mtrilN_in = ctx.inp("mtrilN", (32, 32), BF16)
    mtriuN_in = ctx.inp("mtriuN", (32, 32), BF16)
    mtriuD_in = ctx.inp("mtriuD", (32, 32), BF16)
    mtrilN = ctx.sb("c_mtrilN", (32, 32), BF16)
    mtriuN = ctx.sb("c_mtriuN", (32, 32), BF16)
    mtriuD = ctx.sb("c_mtriuD", (32, 32), BF16)
    for (t, a) in ((mtrilN, mtrilN_in), (mtriuN, mtriuN_in), (mtriuD, mtriuD_in)):
        ctx.ld(t[:], a, t)
    g.feed["mtrilN"] = (-np.tril(np.ones((32, 32), np.float32), -1)).astype(_bf16)
    g.feed["mtriuN"] = (-np.triu(np.ones((32, 32), np.float32), 1)).astype(_bf16)
    g.feed["mtriuD"] = np.triu(np.ones((32, 32), np.float32), 0).astype(_bf16)

    pa = ctx.ps("pa", (128, 512))
    pb = ctx.ps("pb", (128, 512))
    pc = ctx.ps("pc", (128, 512))
    pd = ctx.ps("pd", (128, 512))
    pe = ctx.ps("pe", (128, 512))
    p2 = ctx.ps("p2", (128, 1024))
    pbf = ctx.ps("pbf", (128, 256), BF16)

    xln = [ctx.sb(f"xln{c}", (128, T), BF16) for c in range(8)]
    hq = [ctx.sb(f"hq{i}", (128, T), BF16) for i in range(2)]
    hk = [ctx.sb(f"hk{i}", (128, T), BF16) for i in range(2)]
    hv = [ctx.sb(f"hv{i}", (128, T), BF16) for i in range(2)]
    hkb = [ctx.sb(f"hkb{i}", (128, T), BF16) for i in range(2)]
    accf = ctx.sb("accf", (128, T))
    acc2 = ctx.sb("acc2", (128, T))
    st65 = ctx.sb("st65", (65, T), BF16)
    onesP = ctx.sb("onesP", (65, 512), BF16)
    wqt = ctx.sb("wqt", (128, 1024), BF16)
    wbt = ctx.sb("wbt", (128, 16), BF16)
    cwc = ctx.sb("cwc", (128, 48))
    brow = ctx.sb("brow", (1, T))
    Sf = [ctx.sb(f"Sf{i}", (128, 256)) for i in range(2)]
    Sb = [ctx.sb(f"Sb{i}", (128, 256), BF16) for i in range(2)]
    At2 = [ctx.sb(f"At_{i}", (32, 32), BF16) for i in range(2)]
    AtT2 = [ctx.sb(f"AtT_{i}", (32, 32), BF16) for i in range(2)]
    Pw2 = [{k: ctx.sb(f"Pw{k}_{i}", (32, 32), BF16)
            for k in ("2", "2T", "4", "4T", "8", "8T", "16T")} for i in range(2)]
    UW2 = [ctx.sb(f"UW_{i}", (32, 512)) for i in range(2)]
    UWb2 = [ctx.sb(f"UWb_{i}", (32, 512), BF16) for i in range(2)]
    wiT2 = [ctx.sb(f"wiT_{i}", (128, 64), BF16) for i in range(2)]
    ki_tm2 = [ctx.sb(f"ki_tm_{i}", (32, 256), BF16) for i in range(2)]
    ui_t2 = [ctx.sb(f"ui_t_{i}", (32, 256), BF16) for i in range(2)]
    attnT_t2 = [ctx.sb(f"attnT_t_{i}", (32, 32), BF16) for i in range(2)]
    o_t2 = [ctx.sb(f"o_t_{i}", (32, 256), BF16) for i in range(2)]
    e12c = ctx.sb("e12c", (1, 1))
    ctx.op("vector", lambda e: e.memset(e12c[:], 1e-12), writes=[e12c])
    ctx.op("vector", lambda e: e.memset(onesP[:], 1.0), writes=[onesP])

    for c in range(8):
        ctx.ld(xln[c][:], xT[c * 128:(c + 1) * 128, :], xln[c], eng="gpsimd")
    emit_ln(ctx, g, xln, T, dict(pa=pa, pb=pb, pc=pc, pd=pd, pe=pe, p2=p2,
                                 pbf=pbf, rows=st65, onesP=onesP))
    ctx.ld(cwc[:, 0:16], cwq, cwc, eng="sync")
    ctx.ld(cwc[:, 16:32], cwk, cwc, eng="sync")
    ctx.ld(cwc[:, 32:48], cwv, cwc, eng="sync")
    ctx.ld(wbt[:], wb.rearrange("(c p) f -> p c f", p=128), wbt, eng="gpsimd")

    started = {}

    def accmm(out_ap, lhsT, rhs, in_ts):
        bank = out_ap.tensor
        reg = (int(out_ap.offset) * mybir.dt.size(out_ap.dtype)) // 2048
        key = (id(bank), out_ap.base_partition(), reg)
        st = not started.get(key, False)
        started[key] = True
        return ctx.mm(out_ap, lhsT, rhs, bank, in_ts, start=st, stop=False)

    def reset(bank):
        for k in [k for k in started if k[0] == id(bank)]:
            started[k] = False

    def vec(fn, reads=(), writes=()):
        return ctx.op("vector", fn, reads=reads, writes=writes)

    for lh in range(2):
        for (wsl, dsts, cwo) in ((wq, hq, 0), (wk, hk, 16), (wv, hv, 32)):
            for kc in range(2):
                hc = lh * 2 + kc
                ctx.ld(wqt[:],
                       wsl.rearrange("(c p) f -> p c f", p=128)[:, :, 128 * hc:128 * hc + 128],
                       wqt, eng="gpsimd")
                for nt in range(8):
                    tsl = slice(nt * 512, nt * 512 + 512)
                    reset(pe)
                    for cc in range(8):
                        accmm(pe[:, :], wqt[:, cc * 128:(cc + 1) * 128],
                              xln[cc][:, tsl], [wqt, xln[cc]])
                    vec(lambda e, tsl=tsl: e.tensor_copy(out=accf[:, tsl], in_=pe[:, :]),
                        reads=[pe], writes=[accf])
                w3ap = cwc[:, cwo + hc * 4 + 3:cwo + hc * 4 + 4]
                vec(lambda e, w3ap=w3ap: e.tensor_scalar(
                    out=acc2[:], in0=accf[:], scalar1=w3ap, scalar2=None,
                    op0=ALU.mult), reads=[accf, cwc], writes=[acc2])
                for tap in (2, 1, 0):
                    sh = 3 - tap
                    wtap = cwc[:, cwo + hc * 4 + tap:cwo + hc * 4 + tap + 1]
                    vec(lambda e, sh=sh, wtap=wtap: e.scalar_tensor_tensor(
                        out=acc2[:, sh:T], in0=accf[:, 0:T - sh], scalar=wtap,
                        in1=acc2[:, sh:T], op0=ALU.mult, op1=ALU.add),
                        reads=[accf, cwc], writes=[acc2])
                ctx.op("scalar", lambda e, d=dsts[kc]: e.activation(
                    out=d[:], in_=acc2[:], func=AF.Silu, bias=g.zc[:, 0:1], scale=1.0),
                    reads=[acc2, g.zc], writes=[dsts[kc]])
        for dsts in (hq, hk):
            for kc in range(2):
                sqb = hkb[kc]
                ctx.op("scalar", lambda e, d=dsts[kc], sqb=sqb: e.activation(
                    out=sqb[:], in_=d[:], func=AF.Square, bias=g.zc[:, 0:1],
                    scale=1.0), reads=[dsts[kc], g.zc], writes=[sqb])
                for nt in range(8):
                    tsl = slice(nt * 512, nt * 512 + 512)
                    reset(pd)
                    accmm(pd[0:1, 0:512], g.onesc[:], sqb[:, tsl], [g.onesc, sqb])
                    if kc == 0:
                        vec(lambda e, tsl=tsl: e.tensor_copy(
                            out=brow[0:1, tsl], in_=pd[0:1, 0:512]),
                            reads=[pd], writes=[brow])
                    else:
                        vec(lambda e, tsl=tsl: e.tensor_tensor(
                            out=brow[0:1, tsl], in0=brow[0:1, tsl],
                            in1=pd[0:1, 0:512], op=ALU.add),
                            reads=[pd], writes=[brow])
            ctx.op("scalar", lambda e: e.activation(
                out=brow[0:1, :], in_=brow[0:1, :], func=AF.Ln, bias=e12c[0:1, 0:1],
                scale=1.0), reads=[e12c], writes=[brow])
            ctx.op("scalar", lambda e: e.activation(
                out=brow[0:1, :], in_=brow[0:1, :], func=AF.Exp, bias=g.zc[0:1, 0:1],
                scale=-0.5), reads=[g.zc], writes=[brow])
            vec(lambda e: e.tensor_copy(out=st65[0:1, :], in_=brow[0:1, :]),
                reads=[brow], writes=[st65])
            for nt in range(8):
                tsl = slice(nt * 512, nt * 512 + 512)
                reset(pe)
                accmm(pe[:, :], g.onesr[0:1, 0:128], st65[0:1, tsl], [g.onesr, st65])
                for kc in range(2):
                    vec(lambda e, d=dsts[kc], tsl=tsl: e.tensor_tensor(
                        out=d[:, tsl], in0=d[:, tsl], in1=pe[:, :], op=ALU.mult),
                        reads=[pe], writes=[dsts[kc]])
        for nt in range(8):
            tsl = slice(nt * 512, nt * 512 + 512)
            reset(pd)
            for cc in range(8):
                accmm(pd[0:1, 0:512], wbt[:, cc * 2 + lh:cc * 2 + lh + 1],
                      xln[cc][:, tsl], [wbt, xln[cc]])
            ctx.op("scalar", lambda e, tsl=tsl: e.activation(
                out=brow[0:1, tsl], in_=pd[0:1, 0:512], func=AF.Sigmoid,
                bias=g.zc[0:1, 0:1], scale=1.0), reads=[pd, g.zc], writes=[brow])
        vec(lambda e: e.tensor_copy(out=st65[32:33, :], in_=brow[0:1, :]),
            reads=[brow], writes=[st65])
        for nt in range(8):
            tsl = slice(nt * 512, nt * 512 + 512)
            reset(pe)
            accmm(pe[:, :], onesP[32:33, 0:128], st65[32:33, tsl], [onesP, st65])
            for kc in range(2):
                vec(lambda e, kc=kc, tsl=tsl: e.tensor_tensor(
                    out=hkb[kc][:, tsl], in0=hk[kc][:, tsl], in1=pe[:, :],
                    op=ALU.mult), reads=[hk[kc], pe], writes=[hkb[kc]])
                vec(lambda e, kc=kc, tsl=tsl: e.tensor_tensor(
                    out=hv[kc][:, tsl], in0=hv[kc][:, tsl], in1=pe[:, :],
                    op=ALU.mult), reads=[pe], writes=[hv[kc]])
        if debug0 and lh == 0:
            for kc in range(2):
                for (nm, src) in (("hq", hq), ("hk", hk), ("hv", hv),
                                  ("hkb", hkb)):
                    ctx.st(dbg[nm][kc * 128:kc * 128 + 128, :],
                           src[kc][:, 0:32], src[kc], eng="sync")
        for kc in range(2):
            vec(lambda e, kc=kc: e.memset(Sf[kc][:], 0.0), writes=[Sf[kc]])
            vec(lambda e, kc=kc: e.memset(Sb[kc][:], 0.0), writes=[Sb[kc]])
        def chunk_body(ch, At, AtT, Pw, UW, UWb, wiT, ki_tm, ui_t,
                       attnT_t, o_t, lh=lh):
            csl = slice(ch * 32, ch * 32 + 32)
            reset(pa)
            for kc in range(2):
                accmm(pa[0:32, 0:32], hkb[kc][:, csl], hk[kc][:, csl],
                      [hkb[kc], hk[kc]])
                accmm(pa[0:32, 64:96], hk[kc][:, csl], hkb[kc][:, csl],
                      [hk[kc], hkb[kc]])
            vec(lambda e: e.tensor_tensor(out=At[:], in0=pa[0:32, 0:32],
                                          in1=mtrilN[:], op=ALU.mult),
                reads=[pa, mtrilN], writes=[At])
            vec(lambda e: e.tensor_tensor(out=AtT[:], in0=pa[0:32, 64:96],
                                          in1=mtriuN[:], op=ALU.mult),
                reads=[pa, mtriuN], writes=[AtT])
            if debug0 and lh == 0 and ch == 0:
                ctx.st(dbg["At"][:], At[:], At, eng="sync")
                ctx.st(dbg["AtT"][:], AtT[:], AtT, eng="sync")
            reset(pb)
            ctx.mm(pb[0:32, 0:32], AtT[:], At[:], pb, [AtT, At], start=True, stop=True)
            ctx.mm(pb[0:32, 64:96], At[:], AtT[:], pb, [At, AtT], start=True, stop=True)
            vec(lambda e: e.tensor_copy(out=Pw["2"][:], in_=pb[0:32, 0:32]),
                reads=[pb], writes=[Pw["2"]])
            vec(lambda e: e.tensor_copy(out=Pw["2T"][:], in_=pb[0:32, 64:96]),
                reads=[pb], writes=[Pw["2T"]])
            reset(pb)
            ctx.mm(pb[0:32, 0:32], Pw["2T"][:], Pw["2"][:], pb, [Pw["2T"], Pw["2"]],
                   start=True, stop=True)
            ctx.mm(pb[0:32, 64:96], Pw["2"][:], Pw["2T"][:], pb, [Pw["2"], Pw["2T"]],
                   start=True, stop=True)
            vec(lambda e: e.tensor_copy(out=Pw["4"][:], in_=pb[0:32, 0:32]),
                reads=[pb], writes=[Pw["4"]])
            vec(lambda e: e.tensor_copy(out=Pw["4T"][:], in_=pb[0:32, 64:96]),
                reads=[pb], writes=[Pw["4T"]])
            reset(pb)
            ctx.mm(pb[0:32, 0:32], Pw["4T"][:], Pw["4"][:], pb, [Pw["4T"], Pw["4"]],
                   start=True, stop=True)
            ctx.mm(pb[0:32, 64:96], Pw["4"][:], Pw["4T"][:], pb, [Pw["4"], Pw["4T"]],
                   start=True, stop=True)
            vec(lambda e: e.tensor_copy(out=Pw["8"][:], in_=pb[0:32, 0:32]),
                reads=[pb], writes=[Pw["8"]])
            vec(lambda e: e.tensor_copy(out=Pw["8T"][:], in_=pb[0:32, 64:96]),
                reads=[pb], writes=[Pw["8T"]])
            reset(pb)
            ctx.mm(pb[0:32, 0:32], Pw["8"][:], Pw["8T"][:], pb, [Pw["8"], Pw["8T"]],
                   start=True, stop=True)
            vec(lambda e: e.tensor_copy(out=Pw["16T"][:], in_=pb[0:32, 0:32]),
                reads=[pb], writes=[Pw["16T"]])
            for kc in range(2):
                ctx.op("tensor", lambda e, kc=kc, csl=csl: e.transpose(
                    pbf[0:32, 0:128], hv[kc][:, csl], g.iden[:]),
                    reads=[hv[kc], g.iden], writes=[pbf])
                vec(lambda e, kc=kc: e.tensor_copy(
                    out=UW[:, kc * 128:kc * 128 + 128], in_=pbf[0:32, 0:128]),
                    reads=[pbf], writes=[UW])
                ctx.op("tensor", lambda e, kc=kc, csl=csl: e.transpose(
                    pbf[0:32, 0:128], hkb[kc][:, csl], g.iden[:]),
                    reads=[hkb[kc], g.iden], writes=[pbf])
                vec(lambda e, kc=kc: e.tensor_copy(
                    out=UW[:, 256 + kc * 128:256 + kc * 128 + 128],
                    in_=pbf[0:32, 0:128]), reads=[pbf], writes=[UW])
            if debug0 and lh == 0 and ch == 0:
                vec(lambda e: e.tensor_copy(out=UWb[:], in_=UW[:]),
                    reads=[UW], writes=[UWb])
                ctx.st(dbg["UW0"][:], UWb[:], UWb, eng="sync")
                ctx.st(dbg["Pw2"][:], Pw["2"][:], Pw["2"], eng="sync")
                ctx.st(dbg["Pw16T"][:], Pw["16T"][:], Pw["16T"], eng="sync")
            ninv = 0
            for pk in ("16T", "8T", "4T", "2T", None):
                lt = AtT if pk is None else Pw[pk]
                vec(lambda e: e.tensor_copy(out=UWb[:], in_=UW[:]),
                    reads=[UW], writes=[UWb])
                reset(pc)
                ctx.mm(pc[0:32, :], lt[:], UWb[:], pc, [lt, UWb],
                       start=True, stop=True)
                vec(lambda e: e.tensor_tensor(out=UW[:], in0=UW[:],
                                              in1=pc[0:32, :], op=ALU.add),
                    reads=[pc], writes=[UW])
                ninv += 1
                if debug0 and lh == 0 and ch == 0 and ninv == 1:
                    vec(lambda e: e.tensor_copy(out=UWb[:], in_=UW[:]),
                        reads=[UW], writes=[UWb])
                    ctx.st(dbg["UW1"][:], UWb[:], UWb, eng="sync")
            vec(lambda e: e.tensor_copy(out=UWb[:], in_=UW[:]),
                reads=[UW], writes=[UWb])
            if debug0 and lh == 0 and ch == 0:
                ctx.st(dbg["UWb"][:], UWb[:], UWb, eng="sync")
            for kc in range(2):
                ctx.op("tensor", lambda e, kc=kc: e.transpose(
                    pbf[0:128, 0:32], UWb[0:32, 256 + kc * 128:256 + kc * 128 + 128],
                    g.iden[0:32, 0:32]), reads=[UWb, g.iden], writes=[pbf])
                vec(lambda e, kc=kc: e.tensor_copy(
                    out=wiT[:, kc * 32:kc * 32 + 32], in_=pbf[0:128, 0:32]),
                    reads=[pbf], writes=[wiT])
                ctx.op("tensor", lambda e, kc=kc, csl=csl: e.transpose(
                    pbf[0:32, 0:128], hk[kc][:, csl], g.iden[:]),
                    reads=[hk[kc], g.iden], writes=[pbf])
                vec(lambda e, kc=kc: e.tensor_copy(
                    out=ki_tm[:, kc * 128:kc * 128 + 128], in_=pbf[0:32, 0:128]),
                    reads=[pbf], writes=[ki_tm])
            reset(pd)
            if ch > 0:
                for kc in range(2):
                    accmm(pd[0:32, 0:256], wiT[:, kc * 32:kc * 32 + 32], Sb[kc][:],
                          [wiT, Sb[kc]])
                vec(lambda e: e.tensor_tensor(out=ui_t[:], in0=UWb[:, 0:256],
                                              in1=pd[0:32, 0:256], op=ALU.subtract),
                    reads=[UWb, pd], writes=[ui_t])
            else:
                vec(lambda e: e.tensor_copy(out=ui_t[:], in_=UWb[:, 0:256]),
                    reads=[UWb], writes=[ui_t])
            reset(pa)
            for kc in range(2):
                accmm(pa[0:32, 32:64], hk[kc][:, csl], hq[kc][:, csl],
                      [hk[kc], hq[kc]])
            vec(lambda e: e.tensor_tensor(out=attnT_t[:], in0=pa[0:32, 32:64],
                                          in1=mtriuD[:], op=ALU.mult),
                reads=[pa, mtriuD], writes=[attnT_t])
            if debug0 and lh == 0 and ch == 0:
                ctx.st(dbg["ui"][:], ui_t[:], ui_t, eng="sync")
                ctx.st(dbg["attnT"][:], attnT_t[:], attnT_t, eng="sync")
            reset(pe)
            if ch > 0:
                for kc in range(2):
                    accmm(pe[0:32, 0:256], hq[kc][:, csl], Sb[kc][:],
                          [hq[kc], Sb[kc]])
            accmm(pe[0:32, 0:256], attnT_t[:], ui_t[:], [attnT_t, ui_t])
            vec(lambda e: e.tensor_copy(out=o_t[:], in_=pe[0:32, 0:256]),
                reads=[pe], writes=[o_t])
            if debug0 and lh == 0 and ch == 0:
                ctx.st(dbg["ot"][:], o_t[:], o_t, eng="sync")
            ctx.st(deltao[lh, ch * 32:ch * 32 + 32, :], o_t[:], o_t, eng="sync")
            reset(p2)
            for kc in range(2):
                accmm(p2[:, kc * 256:kc * 256 + 256],
                      ki_tm[:, kc * 128:kc * 128 + 128], ui_t[:], [ki_tm, ui_t])
            for kc in range(2):
                vec(lambda e, kc=kc: e.tensor_tensor(
                    out=Sf[kc][:], in0=Sf[kc][:], in1=p2[:, kc * 256:kc * 256 + 256],
                    op=ALU.add), reads=[p2], writes=[Sf[kc]])
                vec(lambda e, kc=kc: e.tensor_copy(out=Sb[kc][:], in_=Sf[kc][:]),
                    reads=[Sf[kc]], writes=[Sb[kc]])
        for ch in range(128):
            par = ch % 2
            chunk_body(ch, At2[par], AtT2[par], Pw2[par], UW2[par], UWb2[par],
                       wiT2[par], ki_tm2[par], ui_t2[par], attnT_t2[par],
                       o_t2[par])
    return ctx, g


DH2 = 256
TD = T // 2 + DH2
TC2 = T // 2


def build_delta_gate():
    ctx = Ctx("d2")
    xTs = ctx.inp("xTs", (E, TD))
    deltasT = ctx.inp("deltasT", (E, TC2), BF16)
    wv = ctx.inp("wv", (E, E))
    cwv = ctx.inp("cwv", (128, 32))
    fsw = ctx.inp("fsw", (128, 40))
    flw = ctx.inp("flw", (128, 512))
    w1h = ctx.inp("w1h", (E, E))
    w1s = ctx.inp("w1s", (16, E))
    b1 = ctx.inp("b1", (128, 8))
    w2g = ctx.inp("w2g", (E, 4))
    b2g = ctx.inp("b2g", (1, 4))
    ltemp = ctx.inp("ltemp", (1, 1))
    crl = ctx.inp("crl", (1, 4))
    rmsw = ctx.inp("rmsw", (128, 2))
    woF = ctx.inp("woF", (E, E))
    xnew = ctx.outp("xnew", (E, TC2))

    g = emit_consts(ctx)

    pa = ctx.ps("pa", (128, 512))
    pb = ctx.ps("pb", (128, 512))
    pc = ctx.ps("pc", (128, 512))
    pd = ctx.ps("pd", (128, 512))
    pe = ctx.ps("pe", (128, 512))
    pbf = ctx.ps("pbf", (128, 256), BF16)

    xln = [ctx.sb(f"xln{c}", (128, TD), BF16) for c in range(8)]
    vh = [ctx.sb(f"vh{c}", (128, TD), BF16) for c in range(8)]
    lsb = [ctx.sb(f"lsb{i}", (128, TC2), BF16) for i in range(2)]
    llb = [ctx.sb(f"llb{i}", (128, TC2), BF16) for i in range(2)]
    dTb = [ctx.sb(f"dTb{i}", (128, TC2), BF16) for i in range(2)]
    sqb = ctx.sb("sqb", (128, 512), BF16)
    accf = ctx.sb("accf", (128, TD))
    accg = ctx.sb("accg", (128, TC2))
    acc2 = ctx.sb("acc2", (128, TD))
    gstat = ctx.sb("gstat", (16, TC2), BF16)
    r65 = ctx.sb("r65", (65, TC2))
    st65 = ctx.sb("st65", (65, TD), BF16)
    onesP = ctx.sb("onesP", (65, 512), BF16)
    w1hall = ctx.sb("w1hall", (128, 8192), BF16)
    wvt = ctx.sb("wvt", (128, 1024), BF16)
    w1st = ctx.sb("w1st", (16, E), BF16)
    w2gt = ctx.sb("w2gt", (128, 32), BF16)
    cwcc = ctx.sb("cwcc", (128, 32))
    fswc = ctx.sb("fswc", (128, 40))
    flwc = ctx.sb("flwc", (128, 512))
    b1c = ctx.sb("b1c", (128, 8))
    ggt = ctx.sb("ggt", (128, 512), BF16)
    ptm = ctx.sb("ptm", (128, 4))
    m1c = ctx.sb("m1c", (128, 2))
    icol = ctx.sb("icol", (128, 1))
    b2b = ctx.sb("b2b", (128, 4))
    flb = ctx.sb("flb", (128, 4))
    sgcol = ctx.sb("sgcol", (128, 4))
    rmsc = ctx.sb("rmsc", (128, 2))
    xr = ctx.sb("xr", (128, 512))
    ob = ctx.sb("ob", (128, 512))
    tq4 = ctx.sb("tq4", (1, 8))
    tb8 = ctx.sb("tb8", (1, 16), BF16)
    e5c = ctx.sb("e5c", (128, 1))
    onescf = ctx.sb("onescf", (128, 1))
    ctx.op("vector", lambda e: e.memset(onesP[:], 1.0), writes=[onesP])
    ctx.op("vector", lambda e: e.memset(e5c[:], 1e-5), writes=[e5c])
    ctx.op("vector", lambda e: e.memset(onescf[:], 1.0), writes=[onescf])
    ctx.op("vector", lambda e: e.memset(flb[:, 0:2], 0.08), writes=[flb])
    ctx.op("vector", lambda e: e.memset(flb[:, 2:4], 0.0), writes=[flb])

    started = {}

    def accmm(out_ap, lhsT, rhs, in_ts):
        bank = out_ap.tensor
        reg = (int(out_ap.offset) * mybir.dt.size(out_ap.dtype)) // 2048
        key = (id(bank), out_ap.base_partition(), reg)
        st = not started.get(key, False)
        started[key] = True
        return ctx.mm(out_ap, lhsT, rhs, bank, in_ts, start=st, stop=False)

    def reset(bank):
        for k in [k for k in started if k[0] == id(bank)]:
            started[k] = False

    def vec(fn, reads=(), writes=()):
        return ctx.op("vector", fn, reads=reads, writes=writes)

    def act(fn, reads=(), writes=()):
        return ctx.op("scalar", fn, reads=reads, writes=writes)

    for c in range(8):
        ctx.ld(xln[c][:], xTs[c * 128:(c + 1) * 128, :], xln[c], eng="gpsimd")
    emit_ln(ctx, g, xln, TD, dict(pa=pa, pb=pb, pc=pc, pd=pd, pe=pe,
                                  pbf=pbf, rows=st65, onesP=onesP))

    ctx.ld(cwcc[:], cwv, cwcc, eng="sync")
    ctx.ld(fswc[:], fsw, fswc, eng="sync")
    ctx.ld(flwc[:], flw, flwc, eng="sync")
    ctx.ld(b1c[:], b1, b1c, eng="sync")
    ctx.ld(rmsc[:], rmsw, rmsc, eng="sync")
    ctx.ld(w1st[:], w1s, w1st, eng="gpsimd")
    ctx.ld(w2gt[:], w2g.rearrange("(c p) f -> p c f", p=128), w2gt, eng="gpsimd")
    for fc in range(8):
        ctx.ld(w1hall[:, fc * 1024:(fc + 1) * 1024],
               w1h.rearrange("(c p) f -> p c f", p=128)[:, :, fc * 128:fc * 128 + 128],
               w1hall, eng="gpsimd")

    gsc = ctx.sb("gsc", (1, 12))
    ctx.ld(gsc[0:1, 0:1], ltemp, gsc, eng="sync")
    ctx.ld(gsc[0:1, 4:8], b2g, gsc, eng="sync")
    ctx.ld(gsc[0:1, 8:12], crl, gsc, eng="sync")
    act(lambda e: e.activation(out=tq4[0:1, 0:1], in_=gsc[0:1, 0:1], func=AF.Exp,
                               bias=g.zc[0:1, 0:1], scale=1.0),
        reads=[gsc, g.zc], writes=[tq4])
    act(lambda e: e.activation(out=tq4[0:1, 0:1], in_=tq4[0:1, 0:1], func=AF.Ln,
                               bias=onescf[0:1, 0:1], scale=1.0),
        reads=[onescf], writes=[tq4])
    vec(lambda e: e.tensor_scalar(out=tq4[0:1, 0:1], in0=tq4[0:1, 0:1],
                                  scalar1=1e-6, scalar2=None, op0=ALU.add),
        reads=[tq4], writes=[tq4])
    vec(lambda e: e.reciprocal(out=tq4[0:1, 1:2], in_=tq4[0:1, 0:1]),
        reads=[tq4], writes=[tq4])
    vec(lambda e: e.tensor_copy(out=tb8[0:1, 0:1], in_=tq4[0:1, 1:2]),
        reads=[tq4], writes=[tb8])
    vec(lambda e: e.tensor_tensor(out=tq4[0:1, 2:3], in0=tq4[0:1, 1:2],
                                  in1=tb8[0:1, 0:1], op=ALU.subtract),
        reads=[tq4, tb8], writes=[tq4])
    vec(lambda e: e.tensor_copy(out=tb8[0:1, 1:2], in_=tq4[0:1, 2:3]),
        reads=[tq4], writes=[tb8])
    vec(lambda e: e.tensor_copy(out=tb8[0:1, 4:8], in_=gsc[0:1, 4:8]),
        reads=[gsc], writes=[tb8])
    vec(lambda e: e.tensor_tensor(out=tq4[0:1, 4:8], in0=gsc[0:1, 4:8],
                                  in1=tb8[0:1, 4:8], op=ALU.subtract),
        reads=[gsc, tb8], writes=[tq4])
    vec(lambda e: e.tensor_copy(out=tb8[0:1, 8:12], in_=tq4[0:1, 4:8]),
        reads=[tq4], writes=[tb8])
    act(lambda e: e.activation(out=tq4[0:1, 4:8], in_=gsc[0:1, 8:12],
                               func=AF.Sigmoid, bias=g.zc[0:1, 0:1], scale=1.0),
        reads=[gsc, g.zc], writes=[tq4])
    vec(lambda e: e.tensor_copy(out=tb8[0:1, 12:16], in_=tq4[0:1, 4:8]),
        reads=[tq4], writes=[tb8])
    reset(pe)
    ctx.mm(pe[:, 0:16], g.onesr[0:1, 0:128], tb8[0:1, :], pe, [g.onesr, tb8],
           start=True, stop=True)
    gev = ctx.sb("gev", (128, 16))
    vec(lambda e: e.tensor_copy(out=gev[:], in_=pe[:, 0:16]),
        reads=[pe], writes=[gev])
    vec(lambda e: e.tensor_tensor(out=icol[:, 0:1], in0=gev[:, 0:1],
                                  in1=gev[:, 1:2], op=ALU.add),
        reads=[gev], writes=[icol])
    vec(lambda e: e.tensor_tensor(out=b2b[:], in0=gev[:, 4:8], in1=gev[:, 8:12],
                                  op=ALU.add), reads=[gev], writes=[b2b])
    vec(lambda e: e.tensor_copy(out=sgcol[:], in_=gev[:, 12:16]),
        reads=[gev], writes=[sgcol])

    for hc in range(8):
        ctx.ld(wvt[:], wv.rearrange("(c p) f -> p c f", p=128)[:, :, 128 * hc:128 * hc + 128],
               wvt, eng="gpsimd")
        for i0 in range(0, TD, 512):
            w_ = min(512, TD - i0)
            tsl = slice(i0, i0 + w_)
            reset(pe)
            for cc in range(8):
                accmm(pe[:, 0:w_], wvt[:, cc * 128:(cc + 1) * 128], xln[cc][:, tsl],
                      [wvt, xln[cc]])
            vec(lambda e, tsl=tsl, w_=w_: e.tensor_copy(out=accf[:, tsl],
                                                        in_=pe[:, 0:w_]),
                reads=[pe], writes=[accf])
        w3 = cwcc[:, hc * 4 + 3:hc * 4 + 4]
        vec(lambda e, w3=w3: e.tensor_scalar(out=acc2[:], in0=accf[:], scalar1=w3,
                                             scalar2=None, op0=ALU.mult),
            reads=[accf, cwcc], writes=[acc2])
        for tap in (2, 1, 0):
            sh = 3 - tap
            wtap = cwcc[:, hc * 4 + tap:hc * 4 + tap + 1]
            vec(lambda e, sh=sh, wtap=wtap: e.scalar_tensor_tensor(
                out=acc2[:, sh:TD], in0=accf[:, 0:TD - sh], scalar=wtap,
                in1=acc2[:, sh:TD], op0=ALU.mult, op1=ALU.add),
                reads=[accf, cwcc], writes=[acc2])
        act(lambda e, hc=hc: e.activation(out=vh[hc][:], in_=acc2[:], func=AF.Silu,
                                          bias=g.zc[:, 0:1], scale=1.0),
            reads=[acc2, g.zc], writes=[vh[hc]])

    for h in range(4):
        for kc in range(2):
            hc = h * 2 + kc
            for (K, wt, dst) in ((5, fswc, lsb), (64, flwc, llb)):
                for tap in range(K):
                    sh = K - 1 - tap
                    wtap = wt[:, hc * K + tap:hc * K + tap + 1]
                    if tap == 0:
                        vec(lambda e, sh=sh, wtap=wtap, hc=hc: e.tensor_scalar(
                            out=accf[:, 0:TC2], in0=vh[hc][:, DH2 - sh:DH2 - sh + TC2],
                            scalar1=wtap, scalar2=None, op0=ALU.mult),
                            reads=[vh[hc], wt], writes=[accf])
                    else:
                        vec(lambda e, sh=sh, wtap=wtap, hc=hc: e.scalar_tensor_tensor(
                            out=accf[:, 0:TC2], in0=vh[hc][:, DH2 - sh:DH2 - sh + TC2],
                            scalar=wtap, in1=accf[:, 0:TC2], op0=ALU.mult, op1=ALU.add),
                            reads=[vh[hc], wt], writes=[accf])
                vec(lambda e, dst=dst, kc=kc: e.tensor_copy(out=dst[kc][:],
                                                            in_=accf[:, 0:TC2]),
                    reads=[accf], writes=[dst[kc]])
            ctx.ld(dTb[kc][:], deltasT[hc * 128:(hc + 1) * 128, :], dTb[kc],
                   eng="sync")
        for ti in range(4):
            tiles = (lsb, llb, dTb, None)[ti]
            for nt in range(TC2 // 512):
                tsl = slice(nt * 512, nt * 512 + 512)
                reset(pa)
                reset(pb)
                reset(pc)
                for kc in range(2):
                    if ti == 3:
                        src_ap = vh[h * 2 + kc][:, DH2 + nt * 512:DH2 + nt * 512 + 512]
                        src_t = vh[h * 2 + kc]
                    else:
                        src_ap = tiles[kc][:, tsl]
                        src_t = tiles[kc]
                    accmm(pa[0:1, 0:512], g.onesc[:], src_ap, [g.onesc, src_t])
                    act(lambda e, src_ap=src_ap: e.activation(
                        out=sqb[:, 0:512], in_=src_ap, func=AF.Square,
                        bias=g.zc[:, 0:1], scale=1.0), reads=[src_t, g.zc],
                        writes=[sqb])
                    accmm(pb[0:1, 0:512], g.onesc[:], sqb[:, 0:512], [g.onesc, sqb])
                    act(lambda e, src_ap=src_ap: e.activation(
                        out=sqb[:, 0:512], in_=src_ap, func=AF.Abs,
                        bias=g.zc[:, 0:1], scale=1.0), reads=[src_t, g.zc],
                        writes=[sqb])
                    accmm(pc[0:1, 0:512], g.onesc[:], sqb[:, 0:512], [g.onesc, sqb])
                vec(lambda e, tsl=tsl: e.tensor_scalar(
                    out=accf[0:1, tsl], in0=pa[0:1, 0:512], scalar1=1.0 / 256.0,
                    scalar2=None, op0=ALU.mult), reads=[pa], writes=[accf])
                vec(lambda e, tsl=tsl: e.tensor_tensor(
                    out=r65[0:1, tsl], in0=accf[0:1, tsl], in1=accf[0:1, tsl],
                    op=ALU.mult), reads=[accf], writes=[r65])
                vec(lambda e, tsl=tsl: e.scalar_tensor_tensor(
                    out=r65[0:1, tsl], in0=pb[0:1, 0:512], scalar=1.0 / 256.0,
                    in1=r65[0:1, tsl], op0=ALU.mult, op1=ALU.subtract),
                    reads=[pb], writes=[r65])
                vec(lambda e, tsl=tsl: e.tensor_scalar(
                    out=r65[32:33, tsl], in0=pc[0:1, 0:512], scalar1=1.0 / 256.0,
                    scalar2=None, op0=ALU.mult), reads=[pc], writes=[r65])
                act(lambda e, tsl=tsl: e.activation(
                    out=r65[64:65, tsl], in_=pb[0:1, 0:512], func=AF.Sqrt,
                    bias=g.zc[64:65, 0:1], scale=1.0), reads=[pb, g.zc], writes=[r65])
            ctx.sbcp(gstat[4 * ti:4 * ti + 1, :], accf[0:1, 0:TC2], gstat, accf,
                     eng="gpsimd")
            ctx.sbcp(gstat[4 * ti + 1:4 * ti + 2, :], r65[0:1, :], gstat, r65,
                     eng="gpsimd")
            ctx.sbcp(gstat[4 * ti + 2:4 * ti + 3, :], r65[32:33, :], gstat, r65,
                     eng="gpsimd")
            ctx.sbcp(gstat[4 * ti + 3:4 * ti + 4, :], r65[64:65, :], gstat, r65,
                     eng="gpsimd")
        lslot = [(pd, 0), (pd, 64), (pc, 0), (pc, 64)]
        reset(pd)
        reset(pc)
        for fc in range(8):
            wsl_off = fc * 1024
            for nt in range(TC2 // 512):
                tsl = slice(nt * 512, nt * 512 + 512)
                reset(pe)
                for cc in range(8):
                    accmm(pe[:, :],
                          w1hall[:, wsl_off + cc * 128:wsl_off + (cc + 1) * 128],
                          xln[cc][:, DH2 + nt * 512:DH2 + nt * 512 + 512],
                          [w1hall, xln[cc]])
                accmm(pe[:, :], w1st[:, 128 * fc:128 * fc + 128], gstat[:, tsl],
                      [w1st, gstat])
                act(lambda e, fc=fc: e.activation(out=ggt[:], in_=pe[:, :],
                                                  func=AF.Gelu,
                                                  bias=b1c[:, fc:fc + 1], scale=1.0),
                    reads=[pe, b1c], writes=[ggt])
                lb, lbase = lslot[nt]
                accmm(lb[lbase:lbase + 4, 0:512], w2gt[:, fc * 4:fc * 4 + 4],
                      ggt[:], [w2gt, ggt])
        for nt in range(TC2 // 512):
            tsl = slice(nt * 512, nt * 512 + 512)
            lb, lbase = lslot[nt]
            vec(lambda e, lb=lb, lbase=lbase, tsl=tsl: e.tensor_copy(
                out=r65[0:4, tsl], in_=lb[lbase:lbase + 4, 0:512]),
                reads=[lb], writes=[r65])
        for tb in range(TC2 // 128):
            tq = slice(tb * 128, tb * 128 + 128)
            ctx.op("tensor", lambda e, tq=tq: e.transpose(
                pc[0:128, 0:4], r65[0:4, tq], g.idenf[0:4, 0:4]),
                reads=[r65, g.idenf], writes=[pc])
            vec(lambda e: e.tensor_tensor(out=ptm[:], in0=pc[0:128, 0:4],
                                          in1=b2b[:], op=ALU.add),
                reads=[pc, b2b], writes=[ptm])
            vec(lambda e: e.tensor_scalar(out=ptm[:], in0=ptm[:],
                                          scalar1=icol[:, 0:1], scalar2=None,
                                          op0=ALU.mult), reads=[ptm, icol],
                writes=[ptm])
            vec(lambda e: e.reduce_max(out=m1c[:, 0:1], in_=ptm[:], axis=AX.X),
                reads=[ptm], writes=[m1c])
            vec(lambda e: e.tensor_scalar(out=m1c[:, 1:2], in0=m1c[:, 0:1],
                                          scalar1=-1.0, scalar2=None, op0=ALU.mult),
                reads=[m1c], writes=[m1c])
            act(lambda e: e.activation(out=ptm[:], in_=ptm[:], func=AF.Exp,
                                       bias=m1c[:, 1:2], scale=1.0),
                reads=[m1c], writes=[ptm])
            vec(lambda e: e.reduce_sum(out=m1c[:, 0:1], in_=ptm[:], axis=AX.X),
                reads=[ptm], writes=[m1c])
            vec(lambda e: e.reciprocal(out=m1c[:, 1:2], in_=m1c[:, 0:1]),
                reads=[m1c], writes=[m1c])
            vec(lambda e: e.tensor_scalar(out=ptm[:], in0=ptm[:],
                                          scalar1=m1c[:, 1:2], scalar2=None,
                                          op0=ALU.mult), reads=[ptm, m1c],
                writes=[ptm])
            vec(lambda e: e.scalar_tensor_tensor(out=ptm[:], in0=ptm[:],
                                                 scalar=0.84, in1=flb[:],
                                                 op0=ALU.mult, op1=ALU.add),
                reads=[flb], writes=[ptm])
            vec(lambda e, h=h: e.tensor_tensor(out=ptm[:, 0:1], in0=ptm[:, 0:1],
                                               in1=sgcol[:, h:h + 1], op=ALU.add),
                reads=[sgcol], writes=[ptm])
            ctx.op("tensor", lambda e: e.transpose(
                pc[0:4, 0:128], ptm[:], g.idenf[:]),
                reads=[ptm, g.idenf], writes=[pc])
            vec(lambda e, tq=tq: e.tensor_copy(out=r65[32:36, tq],
                                               in_=pc[0:4, 0:128]),
                reads=[pc], writes=[r65])
        mixc = (acc2, accf)
        for kc in range(2):
            for nt in range(TC2 // 512):
                tsl = slice(nt * 512, nt * 512 + 512)
                for k in range(4):
                    ctx.sbcp(st65[0:1, tsl], r65[32 + k:33 + k, tsl], st65, r65,
                             eng="gpsimd")
                    reset(pe)
                    ctx.mm(pe[:, :], g.onesr[0:1, 0:128], st65[0:1, tsl], pe,
                           [g.onesr, st65], start=True, stop=True)
                    if k == 2:
                        b_ap, b_t = dTb[kc][:, tsl], dTb[kc]
                    elif k == 3:
                        b_ap = vh[h * 2 + kc][:, DH2 + nt * 512:DH2 + nt * 512 + 512]
                        b_t = vh[h * 2 + kc]
                    else:
                        b_ap, b_t = (lsb, llb)[k][kc][:, tsl], (lsb, llb)[k][kc]
                    if k == 0:
                        vec(lambda e, b_ap=b_ap, tsl=tsl, kc=kc: e.tensor_tensor(
                            out=mixc[kc][:, tsl], in0=b_ap, in1=pe[:, :],
                            op=ALU.mult), reads=[b_t, pe], writes=[mixc[kc]])
                    else:
                        vec(lambda e, b_ap=b_ap: e.tensor_tensor(
                            out=ob[:], in0=b_ap, in1=pe[:, :], op=ALU.mult),
                            reads=[b_t, pe], writes=[ob])
                        vec(lambda e, tsl=tsl, kc=kc: e.tensor_tensor(
                            out=mixc[kc][:, tsl], in0=mixc[kc][:, tsl], in1=ob[:],
                            op=ALU.add), reads=[ob], writes=[mixc[kc]])
        for nt in range(TC2 // 512):
            tsl = slice(nt * 512, nt * 512 + 512)
            reset(pd)
            for kc in range(2):
                act(lambda e, kc=kc, tsl=tsl: e.activation(
                    out=sqb[:, 0:512], in_=mixc[kc][:, tsl], func=AF.Square,
                    bias=g.zc[:, 0:1], scale=1.0), reads=[mixc[kc], g.zc],
                    writes=[sqb])
                accmm(pd[0:1, 0:512], g.onesc[:], sqb[:, 0:512], [g.onesc, sqb])
            act(lambda e, tsl=tsl: e.activation(
                out=r65[0:1, tsl], in_=pd[0:1, 0:512], func=AF.Ln,
                bias=e5c[0:1, 0:1], scale=1.0 / 256.0), reads=[pd, e5c],
                writes=[r65])
            act(lambda e, tsl=tsl: e.activation(
                out=r65[0:1, tsl], in_=r65[0:1, tsl], func=AF.Exp,
                bias=g.zc[0:1, 0:1], scale=-0.5), reads=[g.zc], writes=[r65])
        ctx.sbcp(st65[0:1, 0:TC2], r65[0:1, :], st65, r65, eng="gpsimd")
        for nt in range(TC2 // 512):
            tsl = slice(nt * 512, nt * 512 + 512)
            reset(pe)
            ctx.mm(pe[:, :], g.onesr[0:1, 0:128], st65[0:1, tsl], pe,
                   [g.onesr, st65], start=True, stop=True)
            for kc in range(2):
                vec(lambda e, kc=kc, tsl=tsl, h=h: e.scalar_tensor_tensor(
                    out=vh[h * 2 + kc][:, DH2 + tsl.start:DH2 + tsl.stop],
                    in0=mixc[kc][:, tsl], scalar=rmsc[:, kc:kc + 1], in1=pe[:, :],
                    op0=ALU.mult, op1=ALU.mult),
                    reads=[mixc[kc], rmsc, pe], writes=[vh[h * 2 + kc]])
    for fc in range(8):
        fsl = slice(fc * 128, fc * 128 + 128)
        ctx.ld(w1hall[:, fc * 1024:(fc + 1) * 1024],
               woF.rearrange("(c p) f -> p c f", p=128)[:, :, fsl],
               w1hall, eng="gpsimd")
    for fc in range(8):
        fsl = slice(fc * 128, fc * 128 + 128)
        for nt in range(TC2 // 512):
            tsl = slice(nt * 512, nt * 512 + 512)
            bank = pa if nt % 2 == 0 else pb
            reset(bank)
            for cc in range(8):
                accmm(bank[:, :],
                      w1hall[:, fc * 1024 + cc * 128:fc * 1024 + (cc + 1) * 128],
                      vh[cc][:, DH2 + nt * 512:DH2 + nt * 512 + 512],
                      [w1hall, vh[cc]])
            ctx.ld(xr[:], xTs[fsl, DH2 + nt * 512:DH2 + nt * 512 + 512], xr,
                   eng="sync")
            vec(lambda e, bank=bank: e.tensor_tensor(out=ob[:], in0=bank[:, :],
                                                     in1=xr[:], op=ALU.add),
                reads=[bank, xr], writes=[ob])
            ctx.st(xnew[fsl, tsl], ob[:], ob, eng="sync")
    return ctx, g


_DELTA_CACHE = {}


def _conv_flat(w, taps):
    return np.ascontiguousarray(
        w.reshape(8, 128, taps).transpose(1, 0, 2).reshape(128, 8 * taps))


def run_delta(xT, Wq, Wk, Wv, Wb, cwq, cwk, cwv, fir_s, fir_l,
              w1, b1, w2, b2, ltemp, crl, rmsw, Wo, d1_only=False):
    if "d1" not in _DELTA_CACHE:
        ctx, g = build_delta_scan()
        _DELTA_CACHE["d1"] = (ctx.finish(), g)
    nc1, g1 = _DELTA_CACHE["d1"]
    in_maps = []
    for c in range(NCORES):
        b, hh = c // 2, c % 2
        m = dict(g1.feed)
        m["xT"] = np.ascontiguousarray(xT[b])
        m["wq"] = np.ascontiguousarray(Wq[:, 512 * hh:512 * hh + 512])
        m["wk"] = np.ascontiguousarray(Wk[:, 512 * hh:512 * hh + 512])
        m["wv"] = np.ascontiguousarray(Wv[:, 512 * hh:512 * hh + 512])
        m["wb"] = np.ascontiguousarray(Wb[:, 2 * hh:2 * hh + 2])
        for (key, cw) in (("cwq", cwq), ("cwk", cwk), ("cwv", cwv)):
            sl = cw[512 * hh:512 * hh + 512]
            m[key] = np.ascontiguousarray(
                sl.reshape(4, 128, 4).transpose(1, 0, 2).reshape(128, 16))
        in_maps.append(m)
    res = run_bass_kernel_spmd(nc1, in_maps, core_ids=list(range(NCORES)))
    d1o = [np.asarray(r["deltao"]) for r in res.results]
    if d1_only:
        return d1o
    if "d2" not in _DELTA_CACHE:
        ctx, g = build_delta_gate()
        _DELTA_CACHE["d2"] = (ctx.finish(), g)
    nc2, g2 = _DELTA_CACHE["d2"]
    cwvm = _conv_flat(cwv, 4)
    fswm = _conv_flat(fir_s.reshape(1024, 5), 5)
    flwm = _conv_flat(fir_l.reshape(1024, 64), 64)
    w1h = np.ascontiguousarray(w1[0:1024])
    w1s = np.ascontiguousarray(w1[1024:1040])
    b1m = np.ascontiguousarray(b1.reshape(8, 128).T)
    rmsm = np.ascontiguousarray(rmsw.reshape(2, 128).T)
    in2 = []
    for c in range(NCORES):
        b, th = c // 2, c % 2
        t0 = th * TC2
        m = dict(g2.feed)
        halo = (np.zeros((E, DH2), np.float32) if th == 0
                else xT[b][:, t0 - DH2:t0])
        m["xTs"] = np.ascontiguousarray(
            np.concatenate([halo, xT[b][:, t0:t0 + TC2]], 1))
        rows = []
        for hh in range(2):
            dd = d1o[2 * b + hh]
            for lh in range(2):
                rows.append(np.ascontiguousarray(dd[lh, t0:t0 + TC2, :].T))
        m["deltasT"] = np.ascontiguousarray(np.concatenate(rows, 0))
        m["wv"] = Wv
        m["cwv"] = cwvm
        m["fsw"] = fswm
        m["flw"] = flwm
        m["w1h"] = w1h
        m["w1s"] = w1s
        m["b1"] = b1m
        m["w2g"] = np.ascontiguousarray(w2)
        m["b2g"] = np.ascontiguousarray(b2.reshape(1, 4))
        m["ltemp"] = np.ascontiguousarray(ltemp.reshape(1, 1))
        m["crl"] = np.ascontiguousarray(crl.reshape(1, 4))
        m["rmsw"] = rmsm
        m["woF"] = np.ascontiguousarray(Wo)
        in2.append(m)
    res2 = run_bass_kernel_spmd(nc2, in2, core_ids=list(range(NCORES)))
    outs = [np.asarray(r["xnew"], np.float32) for r in res2.results]
    return [np.concatenate([outs[2 * b], outs[2 * b + 1]], 1) for b in range(4)]


def kernel(**kw):
    x = np.asarray(kw["L0_attn_a_x"], np.float32)
    xT = [np.ascontiguousarray(x[b].T) for b in range(4)]
    f32 = lambda a: np.asarray(a, np.float32)

    parts = run_attn_a0(x, f32(kw["L0_attn_a_Wq"]), f32(kw["L0_attn_a_bq"]),
                        f32(kw["L0_attn_a_Wk"]), f32(kw["L0_attn_a_bk"]),
                        f32(kw["L0_attn_a_Wv"]), f32(kw["L0_attn_a_bv"]),
                        f32(kw["L0_attn_a_Wo"]), f32(kw["L0_attn_a_bo"]),
                        f32(kw["L0_attn_a_table_k"]), f32(kw["L0_attn_a_table_v"]))
    pA = [parts[2 * b] for b in range(4)]
    pB = [parts[2 * b + 1] for b in range(4)]

    w_in, w_out = f32(kw["L0_ffn_w_in"]), f32(kw["L0_ffn_w_out"])
    k_ffwd, v_ffwd = f32(kw["L0_ffn_k_ffwd"]), f32(kw["L0_ffn_v_ffwd"])
    xc, at = run_ffn0_attn((xT, pA, pB), w_in)
    x1 = run_ffn0_exp(xc, at, w_in, k_ffwd, v_ffwd, w_out)

    parts = run_attn_b0(x1, f32(kw["L0_attn_b_Wq"]), f32(kw["L0_attn_b_bq"]),
                        f32(kw["L0_attn_b_Wk"]), f32(kw["L0_attn_b_bk"]),
                        f32(kw["L0_attn_b_Wv"]), f32(kw["L0_attn_b_bv"]),
                        f32(kw["L0_attn_b_Wo"]), f32(kw["L0_attn_b_bo"]))
    pA = [parts[2 * b] for b in range(4)]
    pB = [parts[2 * b + 1] for b in range(4)]

    xc, at = run_ffn0_attn((x1, pA, pB), w_in)
    x2 = run_ffn0_exp(xc, at, w_in, k_ffwd, v_ffwd, w_out)

    parts = run_mla(x2, f32(kw["L1_attn_a_Wq"]), f32(kw["L1_attn_a_Wqr"]),
                    f32(kw["L1_attn_a_Wkr"]), f32(kw["L1_attn_a_Wkvd"]),
                    f32(kw["L1_attn_a_Wku"]), f32(kw["L1_attn_a_Wvu"]),
                    f32(kw["L1_attn_a_Wo"]), f32(kw["L1_attn_a_lobo_log"]))
    pA = [parts[2 * b] for b in range(4)]
    pB = [parts[2 * b + 1] for b in range(4)]

    wqkv, qs, ks = f32(kw["L1_ffn_wqkv"]), f32(kw["L1_ffn_q_scale"]), f32(kw["L1_ffn_k_scale"])
    wo, lng, lnb = f32(kw["L1_ffn_wo"]), f32(kw["L1_ffn_ln_g"]), f32(kw["L1_ffn_ln_b"])
    w1, b1 = f32(kw["L1_ffn_w1"]), f32(kw["L1_ffn_b1"])
    w2, b2 = f32(kw["L1_ffn_w2"]), f32(kw["L1_ffn_b2"])
    z, xcur = run_l1f(x2, (pA, pB), wqkv[0], qs[0], ks[0], wo[0], lng[0],
                      lnb[0], w1[0], b1[0], w2[0], b2[0],
                      first=True, last_pair=False)
    x3, _ = run_l1f(z, None, wqkv[1], qs[1], ks[1], wo[1], lng[1], lnb[1],
                    w1[1], b1[1], w2[1], b2[1],
                    first=False, last_pair=True, xcur_in=xcur)

    x4 = run_delta(x3, f32(kw["L1_attn_b_Wq"]), f32(kw["L1_attn_b_Wk"]),
                   f32(kw["L1_attn_b_Wv"]), f32(kw["L1_attn_b_Wb"]),
                   f32(kw["L1_attn_b_conv_w_q"]), f32(kw["L1_attn_b_conv_w_k"]),
                   f32(kw["L1_attn_b_conv_w_v"]), f32(kw["L1_attn_b_fir_short_w"]),
                   f32(kw["L1_attn_b_fir_long_w"]), f32(kw["L1_attn_b_mlp_w1"]),
                   f32(kw["L1_attn_b_mlp_b1"]), f32(kw["L1_attn_b_mlp_w2"]),
                   f32(kw["L1_attn_b_mlp_b2"]), f32(kw["L1_attn_b_logit_temp"]),
                   f32(kw["L1_attn_b_conv_res_logit"]), f32(kw["L1_attn_b_rms_w"]),
                   f32(kw["L1_attn_b_Wo"]))

    zz = np.zeros((E, T), np.float32)
    zzl = [zz] * 4
    z, xcur = run_l1f(x4, (zzl, zzl), wqkv[0], qs[0], ks[0], wo[0], lng[0],
                      lnb[0], w1[0], b1[0], w2[0], b2[0],
                      first=True, last_pair=False)
    x5, _ = run_l1f(z, None, wqkv[1], qs[1], ks[1], wo[1], lng[1], lnb[1],
                    w1[1], b1[1], w2[1], b2[1],
                    first=False, last_pair=True, xcur_in=xcur)
    return np.ascontiguousarray(
        np.stack([x5[b].T for b in range(4)])).astype(np.float32)
